# version 17
# baseline (speedup 1.0000x reference)
import contextlib
import numpy as np

import concourse.bass as bass
import concourse.mybir as mybir
from concourse.bass_utils import run_bass_kernel_spmd

F32 = mybir.dt.float32
BF16 = mybir.dt.bfloat16
AF = mybir.ActivationFunctionType
ALU = mybir.AluOpType
AX = mybir.AxisListType

D = 1024
KB = 8
TC = 8192
CH = 256
TB = CH // 128
NCH = TC // CH
NBCH = NCH // 2

EMIT_LOG = []


def build_program(tc=TC, debug=False):
    EMIT_LOG.clear()
    NCH = tc // CH
    NBCH = max(1, NCH // 2)
    nc = bass.Bass("TRN2", target_bir_lowering=False)

    x_h = nc.dram_tensor("x", [tc, D], F32, kind="ExternalInput").ap()
    winT_h = nc.dram_tensor("winT", [D, D], F32, kind="ExternalInput").ap()
    woutT_h = nc.dram_tensor("woutT", [D, D], F32, kind="ExternalInput").ap()
    wT_h = nc.dram_tensor("wT", [D, D], F32, kind="ExternalInput").ap()
    a1_h = nc.dram_tensor("a1", [D, D], F32, kind="ExternalInput").ap()
    a2_h = nc.dram_tensor("a2", [D, D], F32, kind="ExternalInput").ap()
    wmod_h = nc.dram_tensor("wmod", [D, 6 * D], F32, kind="ExternalInput").ap()
    cT_h = nc.dram_tensor("cT", [D, 2], F32, kind="ExternalInput").ap()
    avec_h = nc.dram_tensor("avec", [D], F32, kind="ExternalInput").ap()
    bvec_h = nc.dram_tensor("bvec", [D], F32, kind="ExternalInput").ap()
    bin_h = nc.dram_tensor("binv", [D], F32, kind="ExternalInput").ap()
    bout_h = nc.dram_tensor("boutv", [D], F32, kind="ExternalInput").ap()
    fbias_h = nc.dram_tensor("fbias", [D], F32, kind="ExternalInput").ap()
    ident_h = nc.dram_tensor("ident", [128, 128], F32, kind="ExternalInput").ap()
    ones_h = nc.dram_tensor("ones1", [1, 128], F32, kind="ExternalInput").ap()
    out_h = nc.dram_tensor("out", [tc, D], F32, kind="ExternalOutput").ap()
    t1t_d = nc.dram_tensor("t1t_d", [D, D], F32).ap()
    modscr = nc.dram_tensor("modscr", [2, 6 * D], F32).ap()
    gabscr = nc.dram_tensor("gabscr", [2, D], F32).ap()
    gascr = nc.dram_tensor("gascr", [3, 128, D], F32).ap()

    with contextlib.ExitStack() as stp:
        def sb(name, shape, dt=F32):
            return stp.enter_context(nc.sbuf_tensor(name, shape, dt))

        s_winT_hi = sb("s_winT_hi", [128, KB, D], BF16)
        s_winT_lo = sb("s_winT_lo", [128, KB, D], BF16)
        s_woutT_hi = sb("s_woutT_hi", [128, KB, D], BF16)
        s_woutT_lo = sb("s_woutT_lo", [128, KB, D], BF16)
        s_Wc_hi = sb("s_Wc_hi", [128, KB, D], BF16)
        s_Wc_lo = sb("s_Wc_lo", [128, KB, D], BF16)
        s_id = sb("s_id", [128, 128])
        s_ones = sb("s_ones", [1, 128])
        s_GA = sb("s_GA", [128, D])
        s_GAB = sb("s_GAB", [128, D])
        s_GB = sb("s_GB", [128, D])
        s_BIASC = sb("s_BIASC", [128, D])
        s_ca1 = sb("s_ca1", [128, 16])
        s_sa = sb("s_sa", [128, 16])
        s_cb1 = sb("s_cb1", [128, 16])
        s_sb = sb("s_sb", [128, 16])
        s_A = sb("s_A", [128, KB])
        s_bv = sb("s_bv", [128, KB])
        s_binv = sb("s_binv", [128, KB])
        s_bbi = sb("s_bbi", [128, KB])
        scarry = sb("scarry", [128, KB])
        s_cT = sb("s_cT", [128, KB, 2])
        s_silc = sb("s_silc", [128, KB, 2])
        s_av = sb("s_av", [128, KB])
        s_s1a = sb("s_s1a", [128, TB]); s_s2a = sb("s_s2a", [128, TB])
        s_s1b = sb("s_s1b", [128, TB]); s_s2b = sb("s_s2b", [128, TB])
        s_vara = sb("s_vara", [128, TB]); s_varb = sb("s_varb", [128, TB])
        s_sda = sb("s_sda", [128, TB]); s_sdb = sb("s_sdb", [128, TB])
        s_ra = sb("s_ra", [128, TB]); s_rb = sb("s_rb", [128, TB])
        s_nma = sb("s_nma", [128, TB]); s_nmb = sb("s_nmb", [128, TB])
        s_q2 = sb("s_q2", [128, TB])

        pb = [stp.enter_context(nc.psum_tensor(f"pb{i}", [128, 512], F32))
              for i in range(8)]

        SEMS = {}
        NAME = {}
        def sem(name):
            s = stp.enter_context(nc.semaphore(name))
            SEMS[name] = s
            NAME[id(s)] = name
            return s

        sem_x = sem("sem_x"); sem_st = sem("sem_st"); sem_gat = sem("sem_gat")
        s_v = sem("s_v"); s_vv = sem("s_vv"); s_sq = sem("s_sq")
        s_tr = sem("s_tr"); s_b01 = sem("s_b01"); s_sp = sem("s_sp")
        s_mm = sem("s_mm"); s_b23 = sem("s_b23"); s_b47 = sem("s_b47")
        s_th = sem("s_th"); s_out = sem("s_out")
        s_scan = sem("s_scan")
        u1 = sem("u1"); u2 = sem("u2"); u3 = sem("u3"); u4 = sem("u4")
        d0 = sem("d0"); d1 = sem("d1")
        sem_x0 = sem("sem_x0"); sem_x1 = sem("sem_x1")
        st0 = sem("st0"); st1 = sem("st1")

        def W(eng, engname, smp, val):
            EMIT_LOG.append((engname, "w", NAME[id(smp)], val))
            if val > 0:
                eng.wait_ge(smp, val)

        def I(ins, engname, smp, val=1):
            EMIT_LOG.append((engname, "i", NAME[id(smp)], val))
            return ins.then_inc(smp, val)

        G = {"u1": 0, "u2": 0, "u3": 0, "u4": 0}

        with contextlib.ExitStack() as st2:
            def sb2(name, shape, dt=F32):
                return st2.enter_context(nc.sbuf_tensor(name, shape, dt))
            B1 = sb2("B1", [128, KB, D])
            B2h = sb2("B2h", [128, KB, D], BF16)
            B2l = sb2("B2l", [128, KB, D], BF16)
            B3h = s_woutT_hi
            B3l = s_woutT_lo
            WCSh = s_winT_hi
            WCSl = s_winT_lo
            s_rowm = sb2("s_rowm", [2, 512])
            u5 = sem("u5")
            u6 = sem("u6")

            wmod_r = wmod_h.rearrange("(k p) m -> p k m", p=128)
            EMIT_LOG.append(("__BLOCK__", "bar", "", 0))
            with nc.Block() as blk:
                @blk.sync
                def _(sy):
                    small = [
                        (s_id[:], ident_h),
                        (s_ones[:], ones_h),
                        (s_cT[:], cT_h.rearrange("(k p) b -> p k b", p=128)),
                        (s_av[:], avec_h.rearrange("(k p) -> p k", p=128)),
                        (s_bv[:], bvec_h.rearrange("(k p) -> p k", p=128)),
                        (s_binv[:], bin_h.rearrange("(k p) -> p k", p=128)),
                    ]
                    with nc.allow_non_contiguous_dma(reason="small setup vectors"):
                        for dst, srcap in small:
                            I(sy.dma_start(out=dst, in_=srcap), "sync", u1, 16)
                    I(sy.dma_start(out=B1[:],
                                   in_=a1_h.rearrange("(k p) c -> p k c", p=128)),
                      "sync", u6, 16)
                    W(sy, "sync", u2, 1)
                    I(sy.dma_start(out=B1[:],
                                   in_=wT_h.rearrange("(k p) c -> p k c", p=128)),
                      "sync", u6, 16)
                    W(sy, "sync", u2, 2)
                    I(sy.dma_start(out=B1[:],
                                   in_=a2_h.rearrange("(k p) c -> p k c", p=128)),
                      "sync", u6, 16)
                    for n in range(12):
                        W(sy, "sync", u2, 19)
                        if n >= 2:
                            W(sy, "sync", u3, 32 + n - 1)
                        I(sy.dma_start(
                            out=B1[:, :, (n % 2) * 512:(n % 2) * 512 + 512],
                            in_=wmod_r[:, :, n * 512:(n + 1) * 512]), "sync",
                          [d0, d1][n % 2], 16)

                @blk.scalar
                def _(sc):
                    W(sc, "scalar", u1, 16 * 6)
                    sc.activation(s_silc[:], s_cT[:], AF.Silu)
                    I(sc.activation(s_A[:], s_av[:], AF.Sigmoid), "scalar", u5)

                @blk.vector
                def _(ve):
                    def vsplit(hi, lo, src):
                        ve.tensor_copy(hi[:], src[:])
                        ve.drain()
                        return ve.tensor_tensor(lo[:], src[:], hi[:], ALU.subtract)
                    W(ve, "vector", u1, 16 * 6)
                    W(ve, "vector", u6, 16)
                    I(vsplit(B2h, B2l, B1), "vector", u2)
                    W(ve, "vector", u6, 32)
                    I(vsplit(B3h, B3l, B1), "vector", u2)
                    ve.tensor_tensor(s_bbi[:], s_bv[:], s_binv[:], ALU.mult)
                    for m in range(16):
                        j, n = m // 2, m % 2
                        W(ve, "vector", u3, m + 1)
                        hi = s_Wc_hi[:, j, n * 512:(n + 1) * 512]
                        lo = s_Wc_lo[:, j, n * 512:(n + 1) * 512]
                        ve.tensor_copy(hi, pb[m % 2][:])
                        ve.drain()
                        I(ve.tensor_tensor(lo, pb[m % 2][:], hi, ALU.subtract),
                          "vector", u2)
                    W(ve, "vector", u6, 48)
                    W(ve, "vector", u3, 16)
                    I(vsplit(B2h, B2l, B1), "vector", u2)
                    for m in range(16):
                        j, n = m // 2, m % 2
                        W(ve, "vector", u3, 16 + m + 1)
                        hi = WCSh[:, j, n * 512:(n + 1) * 512]
                        lo = WCSl[:, j, n * 512:(n + 1) * 512]
                        ve.tensor_copy(hi, pb[m % 2][:])
                        ve.drain()
                        I(ve.tensor_tensor(lo, pb[m % 2][:], hi, ALU.subtract),
                          "vector", u2)
                    W(ve, "vector", u3, 32)
                    ve.drain()
                    I(ve.tensor_copy(s_Wc_hi[:], WCSh[:]), "vector", u2)
                    I(ve.tensor_copy(s_Wc_lo[:], WCSl[:]), "vector", u2)
                    for n in range(12):
                        W(ve, "vector", u3, 32 + n + 1)
                        if n >= 1:
                            W(ve, "vector", u4, 16 * n)
                        I(ve.tensor_copy(s_rowm[:], pb[n % 2][0:2, :]),
                          "vector", u2)

                @blk.tensor
                def _(te):
                    for m in range(16):
                        j, n = m // 2, m % 2
                        W(te, "tensor", u2, max(2, m + 1))
                        last = None
                        first = True
                        for lh, rh in [(B2h, B3h), (B2h, B3l), (B2l, B3h)]:
                            for k in range(KB):
                                last = te.matmul(
                                    pb[m % 2][:], lh[:, k, j * 128:(j + 1) * 128],
                                    rh[:, k, n * 512:(n + 1) * 512],
                                    start=first, stop=(lh is B2l and k == KB - 1))
                                first = False
                        I(last, "tensor", u3)
                    for m in range(16):
                        j, n = m // 2, m % 2
                        W(te, "tensor", u2, max(19, m + 18))
                        last = None
                        first = True
                        for lh, rh in [(s_Wc_hi, B2h), (s_Wc_hi, B2l),
                                       (s_Wc_lo, B2h)]:
                            for k in range(KB):
                                last = te.matmul(
                                    pb[m % 2][:], lh[:, k, j * 128:(j + 1) * 128],
                                    rh[:, k, n * 512:(n + 1) * 512],
                                    start=first, stop=(lh is s_Wc_lo and k == KB - 1))
                                first = False
                        I(last, "tensor", u3)
                    W(te, "tensor", u5, 1)
                    for n in range(12):
                        W(te, "tensor", [d0, d1][n % 2], 16 * (n // 2 + 1))
                        W(te, "tensor", u2, 36 + n if n >= 2 else 34 + n)
                        last = None
                        for k in range(KB):
                            last = te.matmul(
                                pb[n % 2][0:2, :], s_silc[:, k, :],
                                B1[:, k, (n % 2) * 512:(n % 2) * 512 + 512],
                                start=(k == 0), stop=(k == KB - 1))
                        I(last, "tensor", u3)

                @blk.gpsimd
                def _(gp):
                    for n in range(12):
                        W(gp, "gpsimd", u2, 38 + n)
                        I(gp.dma_start(out=modscr[:, n * 512:(n + 1) * 512],
                                       in_=s_rowm[:]), "gpsimd", u4, 16)
                    W(gp, "gpsimd", u4, 16 * 12)
            G["u1"] = 16 * 6; G["u2"] = 49; G["u3"] = 44; G["u4"] = 16 * 12

        with contextlib.ExitStack() as st2:
            def sb2(name, shape, dt=F32):
                return st2.enter_context(nc.sbuf_tensor(name, shape, dt))
            B4 = sb2("B4", [128, KB, D])
            s_rga = sb2("s_rga", [2, D])
            s_rgb = sb2("s_rgb", [2, D])
            s_bout2 = sb2("s_bout2", [2, D])
            s_row2 = sb2("s_row2", [2, D])
            s_mask = sb2("s_mask", [2, D])
            s_sgn = sb2("s_sgn", [2, D])
            s_row1 = s_row2[0:1, :]
            s_rgab = s_sgn

            b5_u1, b5_u2, b5_u4 = G["u1"], G["u2"], G["u4"]
            EMIT_LOG.append(("__BLOCK__", "bar", "", 0))
            with nc.Block() as blk:
                @blk.sync
                def _(sy):
                    I(sy.dma_start(out=s_rga[:], in_=modscr[:, 2 * D:3 * D]),
                      "sync", u1, 16)
                    I(sy.dma_start(out=s_rgb[:], in_=modscr[:, 5 * D:6 * D]),
                      "sync", u1, 16)
                    I(sy.dma_start(out=s_bout2[0:1, :],
                                   in_=bout_h.rearrange("(a d) -> a d", a=1)),
                      "sync", u1, 16)
                    I(sy.dma_start(out=s_bout2[1:2, :],
                                   in_=bout_h.rearrange("(a d) -> a d", a=1)),
                      "sync", u1, 16)

                @blk.vector
                def _(ve):
                    W(ve, "vector", u1, b5_u1 + 64)
                    for rr in (s_rga, s_rgb):
                        ve.tensor_single_scalar(s_mask[:], rr[:], 0.0, ALU.is_ge)
                        ve.drain()
                        ve.tensor_scalar(s_sgn[:], s_mask[:], 2.0, -1.0,
                                         ALU.mult, ALU.add)
                        ve.drain()
                        ve.tensor_tensor(s_mask[:], rr[:], s_sgn[:], ALU.mult)
                        ve.drain()
                        ve.tensor_scalar_max(s_mask[:], s_mask[:], 0.25)
                        ve.drain()
                        ve.tensor_tensor(rr[:], s_mask[:], s_sgn[:], ALU.mult)
                        ve.drain()
                    I(ve.tensor_tensor(s_rgab[:], s_rga[:], s_bout2[:], ALU.mult),
                      "vector", u2)

                @blk.gpsimd
                def _(gp):
                    W(gp, "gpsimd", u2, b5_u2 + 1)
                    I(gp.dma_start(out=modscr[:, 2 * D:3 * D], in_=s_rga[:]),
                      "gpsimd", u4, 16)
                    I(gp.dma_start(out=modscr[:, 5 * D:6 * D], in_=s_rgb[:]),
                      "gpsimd", u4, 16)
                    I(gp.dma_start(out=gabscr[:], in_=s_rgab[:]), "gpsimd", u4, 16)
                    W(gp, "gpsimd", u4, b5_u4 + 48)
            G["u1"] += 64; G["u2"] += 1; G["u4"] += 48

            bc_jobs = [
                (s_GA, 0, modscr[1, 2 * D:3 * D]),
                (s_GAB, 1, gabscr[1, :]),
                (s_GB, 2, modscr[1, 5 * D:6 * D]),
                (s_GA, None, modscr[0, 2 * D:3 * D]),
                (s_GAB, None, gabscr[0, :]),
                (s_GB, None, modscr[0, 5 * D:6 * D]),
                (s_BIASC, None, fbias_h),
            ]
            b6_u1, b6_u2, b6_u3, b6_u4 = G["u1"], G["u2"], G["u3"], G["u4"]
            EMIT_LOG.append(("__BLOCK__", "bar", "", 0))
            with nc.Block() as blk:
                @blk.sync
                def _(sy):
                    for q, (_, _, srow) in enumerate(bc_jobs):
                        if q >= 1:
                            W(sy, "sync", u3, b6_u3 + 2 * q)
                        I(sy.dma_start(out=s_row1,
                                       in_=srow.rearrange("(a d) -> a d", a=1)),
                          "sync", u1, 16)

                @blk.tensor
                def _(te):
                    m = 0
                    for q in range(len(bc_jobs)):
                        W(te, "tensor", u1, b6_u1 + 16 * (q + 1))
                        for n in range(2):
                            if m >= 2:
                                W(te, "tensor", u2, b6_u2 + m - 1)
                            I(te.matmul(pb[m % 2][:], s_ones[:],
                                        s_row1[:, n * 512:(n + 1) * 512],
                                        start=True, stop=True), "tensor", u3)
                            m += 1

                @blk.vector
                def _(ve):
                    m = 0
                    for q, (dst, slot, _) in enumerate(bc_jobs):
                        for n in range(2):
                            W(ve, "vector", u3, b6_u3 + m + 1)
                            if q == 3 and n == 0:
                                W(ve, "vector", u4, b6_u4 + 48)
                            I(ve.tensor_copy(dst[:, n * 512:(n + 1) * 512],
                                             pb[m % 2][:]), "vector", u2)
                            m += 1

                @blk.gpsimd
                def _(gp):
                    for q, (dst, slot, _) in enumerate(bc_jobs):
                        if slot is None:
                            continue
                        W(gp, "gpsimd", u2, b6_u2 + 2 * (q + 1))
                        I(gp.dma_start(out=gascr[slot], in_=dst[:]), "gpsimd", u4, 16)
                    W(gp, "gpsimd", u4, b6_u4 + 48)
            G["u1"] += 16 * 7; G["u2"] += 14; G["u3"] += 14; G["u4"] += 48

            B4b = sb2("B4b", [128, KB, D])
            b7_u1 = G["u1"]
            EMIT_LOG.append(("__BLOCK__", "bar", "", 0))
            with nc.Block() as blk:
                @blk.sync
                def _(sy):
                    I(sy.dma_start(out=B4[:],
                                   in_=winT_h.rearrange("(k p) e -> p k e", p=128)),
                      "sync", u1, 16)

                @blk.gpsimd
                def _(gp):
                    I(gp.dma_start(out=B4b[:],
                                   in_=woutT_h.rearrange("(k p) e -> p k e", p=128)),
                      "gpsimd", u4, 16)

                @blk.vector
                def _(ve):
                    W(ve, "vector", u1, b7_u1 + 16)
                    ve.tensor_copy(s_winT_hi[:], B4[:])
                    ve.drain()
                    ve.tensor_tensor(s_winT_lo[:], B4[:], s_winT_hi[:], ALU.subtract)
                    W(ve, "vector", u4, G["u4"] + 16)
                    ve.tensor_copy(s_woutT_hi[:], B4b[:])
                    ve.drain()
                    ve.tensor_tensor(s_woutT_lo[:], B4b[:], s_woutT_hi[:],
                                     ALU.subtract)
            G["u1"] += 16
            G["u4"] += 16

            b9_u1 = G["u1"]
            EMIT_LOG.append(("__BLOCK__", "bar", "", 0))
            with nc.Block() as blk:
                @blk.sync
                def _(sy):
                    with nc.allow_non_contiguous_dma(reason="small mod vectors"):
                        for bi in range(2):
                            for dst, off in [(s_sa, 0), (s_ca1, 1),
                                             (s_sb, 3), (s_cb1, 4)]:
                                srcv = modscr[bi, off * D:(off + 1) * D]
                                I(sy.dma_start(
                                    out=dst[:, bi * 8:(bi + 1) * 8],
                                    in_=srcv.rearrange("(k p) -> p k", p=128)),
                                  "sync", u1, 16)
                    W(sy, "sync", u1, b9_u1 + 16 * 8)

                @blk.vector
                def _(ve):
                    W(ve, "vector", u1, b9_u1 + 16 * 8)
                    ve.tensor_scalar_add(s_ca1[:], s_ca1[:], 1.0)
                    ve.drain()
                    ve.tensor_scalar_add(s_cb1[:], s_cb1[:], 1.0)
            G["u1"] += 16 * 8

        if debug:
            dbg_specs = [
                ("s_A", s_A, F32), ("s_bbi", s_bbi, F32), ("s_silc", s_silc, F32),
                ("s_ca1", s_ca1, F32), ("s_sa", s_sa, F32),
                ("s_cb1", s_cb1, F32), ("s_sb", s_sb, F32),
                ("s_GA", s_GA, F32), ("s_GAB", s_GAB, F32), ("s_GB", s_GB, F32),
                ("s_BIASC", s_BIASC, F32),
                ("s_winT_hi", s_winT_hi, BF16), ("s_winT_lo", s_winT_lo, BF16),
                ("s_woutT_hi", s_woutT_hi, BF16),
                ("s_Wc_hi", s_Wc_hi, BF16), ("s_Wc_lo", s_Wc_lo, BF16),
                ("s_id", s_id, F32), ("s_ones", s_ones, F32),
            ]
            dbg_outs = {}
            for nm, t, dt_ in dbg_specs:
                shp = list(t.shape) if hasattr(t, "shape") else None
                ap = t[:]
                dbg_outs[nm] = nc.dram_tensor("dbg_" + nm, list(ap.shape), dt_,
                                              kind="ExternalOutput").ap()
            EMIT_LOG.append(("__BLOCK__", "bar", "", 0))
            with nc.Block() as blk:
                @blk.gpsimd
                def _(gp):
                    n = 0
                    for nm, t, dt_ in dbg_specs:
                        I(gp.dma_start(out=dbg_outs[nm], in_=t[:]), "gpsimd", u4, 16)
                        n += 1
                    W(gp, "gpsimd", u4, G["u4"] + 16 * n)
            G["u4"] += 16 * len(dbg_specs)

        sx = sb("sx", [128, 3, TB, D])
        sLNxA = sb("sLNxA", [128, TB, D])
        sLNxB = sb("sLNxB", [128, TB, D])
        sLNxm1 = sb("sLNxm1", [128, KB, CH])
        sLNxm2 = sb("sLNxm2", [128, KB, CH])
        shi1 = sb("shi1", [128, KB, CH], BF16)
        slo1 = sb("slo1", [128, KB, CH], BF16)
        shi2 = sb("shi2", [128, KB, CH], BF16)
        slo2 = sb("slo2", [128, KB, CH], BF16)
        su = sb("su", [128, 2, CH])
        stmp = sb("stmp", [128, 512])
        s_dum = sb("s_dum", [128, 1])
        sy_t = sb("sy_t", [128, KB, CH])
        shiY = sb("shiY", [128, KB, CH], BF16)
        sloY = sb("sloY", [128, KB, CH], BF16)

        x_r = x_h.rearrange("(c tb p) d -> c p tb d", p=128, tb=TB)
        out_r = out_h.rearrange("(c tb p) d -> c p tb d", p=128, tb=TB)

        sy_flat = sy_t[:].rearrange("p a b -> p (a b)")

        class EV:
            def __init__(self):
                self.counts = {}
                self.events = {}
                self.plan = True
                self.eng = None
                self.engname = ""

            def op(self, f):
                if not self.plan:
                    f()

            def inc(self, sem, f, key=None, by=1):
                nm = NAME[id(sem)]
                if self.plan:
                    v = self.counts.get(nm, 0) + by
                    self.counts[nm] = v
                    if key is not None:
                        assert key not in self.events, key
                        self.events[key] = (sem, v)
                else:
                    EMIT_LOG.append((self.engname, "i", nm, by))
                    f().then_inc(sem, by)

            def wait(self, key):
                if key is None or self.plan:
                    return
                sem, v = self.events[key]
                EMIT_LOG.append((self.engname, "w", NAME[id(sem)], v))
                self.eng.wait_ge(sem, v)

        E = EV()
        NPER = NCH + 1

        seq01 = [("tr1", 0)]
        for k in range(NPER):
            if 0 <= k - 1 < NCH:
                seq01.append(("tr2", k - 1))
            if k + 1 < NCH:
                seq01.append(("tr1", k + 1))
        EVN = {"tr1": "evA", "tr2": "ev2"}
        prev01 = {}
        lastb = {0: None, 1: None}
        for kind, kk in seq01:
            for j in range(KB):
                prev01[(kind, kk, j)] = lastb[j % 2]
                lastb[j % 2] = (EVN[kind], kk, j)

        def prev23(k, i):
            if i >= 2:
                return ("evU", k, i - 2)
            if k >= 1:
                return ("evU", k - 1, 6 + i)
            return None

        def v_LN1(E, ve, k):
            E.wait(("xload", k))
            for tb in range(TB):
                E.op(lambda tb=tb: ve.reduce_sum(
                    s_s1a[:, tb:tb + 1], sx[:, k % 3, tb, :], axis=AX.X))
            E.op(lambda: ve.drain())
            E.wait(("sq1", k))
            E.op(lambda: ve.tensor_scalar_mul(s_nma[:], s_s1a[:], 1.0 / D))
            E.op(lambda: ve.tensor_scalar(s_vara[:], s_s2a[:], 1.0 / D, 1e-5,
                                          ALU.mult, ALU.add))
            E.op(lambda: ve.drain())
            E.op(lambda: ve.tensor_tensor(s_q2[:], s_nma[:], s_nma[:], ALU.mult))
            E.op(lambda: ve.drain())
            E.inc(s_vv, lambda: ve.tensor_tensor(s_vara[:], s_vara[:], s_q2[:],
                                                 ALU.subtract), ("vv1", k))
            E.wait(("sqrt1", k))
            E.op(lambda: ve.reciprocal(s_ra[:], s_sda[:]))
            E.op(lambda: ve.drain())
            E.op(lambda: ve.scalar_tensor_tensor(s_nma[:], s_nma[:], -1.0, s_ra[:],
                                                 ALU.mult, ALU.mult))
            E.op(lambda: ve.drain())
            if k >= 1:
                E.wait(("tr1", k - 1, 7))
            for tb in range(TB):
                f = (lambda tb=tb: ve.tensor_scalar(
                    sLNxB[:, tb, :], sx[:, k % 3, tb, :],
                    s_ra[:, tb:tb + 1], s_nma[:, tb:tb + 1], ALU.mult, ALU.add))
                if tb < TB - 1:
                    E.op(f)
                else:
                    E.inc(s_v, f, ("c1", k))

        def v_LN2(E, ve, k):
            for tb in range(TB):
                E.op(lambda tb=tb: ve.reduce_sum(
                    s_s1b[:, tb:tb + 1], sx[:, k % 3, tb, :], axis=AX.X))
            E.op(lambda: ve.drain())
            E.wait(("sq2", k))
            E.op(lambda: ve.tensor_scalar_mul(s_nmb[:], s_s1b[:], 1.0 / D))
            E.op(lambda: ve.tensor_scalar(s_varb[:], s_s2b[:], 1.0 / D, 1e-5,
                                          ALU.mult, ALU.add))
            E.op(lambda: ve.drain())
            E.op(lambda: ve.tensor_tensor(s_q2[:], s_nmb[:], s_nmb[:], ALU.mult))
            E.op(lambda: ve.drain())
            E.inc(s_vv, lambda: ve.tensor_tensor(s_varb[:], s_varb[:], s_q2[:],
                                                 ALU.subtract), ("vv2", k))
            E.wait(("sqrt2", k))
            E.op(lambda: ve.reciprocal(s_rb[:], s_sdb[:]))
            E.op(lambda: ve.drain())
            E.op(lambda: ve.scalar_tensor_tensor(s_nmb[:], s_nmb[:], -1.0, s_rb[:],
                                                 ALU.mult, ALU.mult))
            E.op(lambda: ve.drain())
            if k >= 1:
                E.wait(("tr2", k - 1, 7))
            for tb in range(TB):
                f = (lambda tb=tb: ve.tensor_scalar(
                    sLNxA[:, tb, :], sx[:, k % 3, tb, :],
                    s_rb[:, tb:tb + 1], s_nmb[:, tb:tb + 1], ALU.mult, ALU.add))
                if tb < TB - 1:
                    E.op(f)
                else:
                    E.inc(s_v, f, ("c2", k))

        def v_split(E, ve, src, hi, lo, key):
            E.op(lambda: ve.tensor_copy(hi[:], src[:]))
            E.op(lambda: ve.drain())
            E.inc(s_sp, lambda: ve.tensor_tensor(lo[:], src[:], hi[:],
                                                 ALU.subtract), key)

        def v_scan(E, ve, k, i0, i1):
            if i0 == 0 and k % NBCH == 0:
                E.op(lambda: ve.memset(scarry[:], 0.0))
                E.op(lambda: ve.drain())
            for i in range(i0, i1):
                E.wait(("evU", k, i))
                E.inc(s_scan, lambda i=i: ve.tensor_tensor_scan(
                    sy_t[:, i, :], s_A[:, i:i + 1].broadcast_to([128, CH]),
                    su[:, i % 2, :], scarry[:, i:i + 1], ALU.mult, ALU.add),
                    ("scan", k, i))
            if i1 == KB:
                E.op(lambda: ve.drain())
                for i in range(KB):
                    E.op(lambda i=i: ve.tensor_copy(
                        scarry[:, i:i + 1], sy_t[:, i, CH - 1:CH]))
                E.op(lambda: ve.drain())

        def v_epiA(E, ve, k):
            if k == NBCH:
                E.wait(("gat",))
            for tb in range(TB):
                E.wait(("mm2", k, tb))
                for n in range(2):
                    E.inc(s_b47, lambda tb=tb, n=n: ve.tensor_tensor(
                        stmp[:],
                        pb[4 + 2 * tb + n][:],
                        s_GA[:, n * 512:(n + 1) * 512], ALU.mult),
                        ("epiAp", k, tb) if n == 1 else None)
                    E.op(lambda: ve.drain())
                    E.op(lambda tb=tb, n=n: ve.tensor_tensor(
                        sx[:, k % 3, tb, n * 512:(n + 1) * 512], stmp[:],
                        sx[:, k % 3, tb, n * 512:(n + 1) * 512], ALU.add))
                    E.op(lambda: ve.drain())
                f = (lambda tb=tb: ve.tensor_tensor(
                    sx[:, k % 3, tb, :], sx[:, k % 3, tb, :], s_GAB[:], ALU.add))
                if tb < TB - 1:
                    E.op(f)
                else:
                    E.inc(s_v, f, ("x2", k))
            E.op(lambda: ve.drain())

        def v_epiF(E, ve, k):
            E.op(lambda: ve.drain())
            for tb in range(TB):
                E.wait(("mm3", k, tb))
                for n in range(2):
                    E.inc(s_b47, lambda tb=tb, n=n: ve.tensor_tensor(
                        sy_flat[:, tb * D + n * 512:tb * D + (n + 1) * 512],
                        pb[4 + 2 * tb + n][:],
                        s_BIASC[:, n * 512:(n + 1) * 512], ALU.add),
                        ("epiFp", k, tb) if n == 1 else None)
            E.wait(("sq2", k))
            for tb in range(TB):
                E.wait(("tanh", k, tb))
                E.op(lambda tb=tb: ve.tensor_tensor(
                    sy_flat[:, tb * D:(tb + 1) * D],
                    sy_flat[:, tb * D:(tb + 1) * D], s_GB[:], ALU.mult))
                E.op(lambda: ve.drain())
                E.inc(s_out, lambda tb=tb: ve.tensor_tensor(
                    sx[:, k % 3, tb, :], sy_flat[:, tb * D:(tb + 1) * D],
                    sx[:, k % 3, tb, :], ALU.add),
                    ("out", k, tb))

        def t_TR(E, te, kind, k):
            E.wait(("c1" if kind == "tr1" else "c2", k))
            buf = sLNxB if kind == "tr1" else sLNxA
            for j in range(KB):
                E.wait(prev01[(kind, k, j)])
                for tb in range(TB):
                    f = (lambda j=j, tb=tb, buf=buf: te.transpose(
                        pb[j % 2][:, tb * 128:(tb + 1) * 128],
                        buf[:, tb, j * 128:(j + 1) * 128], s_id[:]))
                    if tb < TB - 1:
                        E.op(f)
                    else:
                        E.inc(s_tr, f, (kind, k, j))

        def t_MM1(E, te, k):
            E.wait(("spA", k))
            for i in range(KB):
                E.wait(prev23(k, i))
                p = pb[2 + (i % 2)]
                nmm = 3 * KB
                m = 0
                for wt, at in [(s_winT_hi, shi1), (s_winT_hi, slo1),
                               (s_winT_lo, shi1)]:
                    for kk in range(KB):
                        m += 1
                        f = (lambda i=i, kk=kk, wt=wt, at=at, p=p,
                             st=(m == 1), sp=(m == nmm): te.matmul(
                            p[:, 0:CH], wt[:, kk, i * 128:(i + 1) * 128],
                            at[:, kk, :], start=st, stop=sp))
                        if m < nmm:
                            E.op(f)
                        else:
                            E.inc(s_mm, f, ("mm1", k, i))

        def t_MM2(E, te, k):
            E.wait(("spY", k))
            for tb in range(TB):
                E.wait(("epiFp", k - 1, tb) if k >= 1 else None)
                m = 0
                nmm = 2 * 3 * KB
                for n in range(2):
                    p = pb[4 + 2 * tb + n]
                    first = True
                    for yt, wt in [(shiY, s_woutT_hi), (shiY, s_woutT_lo),
                                   (sloY, s_woutT_hi)]:
                        for kk in range(KB):
                            m += 1
                            f = (lambda tb=tb, n=n, kk=kk, yt=yt, wt=wt, p=p,
                                 st=first, sp=(yt is sloY and kk == KB - 1):
                                 te.matmul(p[:],
                                           yt[:, kk, tb * 128:(tb + 1) * 128],
                                           wt[:, kk, n * 512:(n + 1) * 512],
                                           start=st, stop=sp))
                            first = False
                            if m < nmm:
                                E.op(f)
                            else:
                                E.inc(s_mm, f, ("mm2", k, tb))

        def t_MM3(E, te, k):
            E.wait(("sp2", k))
            for tb in range(TB):
                E.wait(("epiAp", k, tb))
                m = 0
                nmm = 2 * 3 * KB
                for n in range(2):
                    p = pb[4 + 2 * tb + n]
                    first = True
                    for at, wt in [(shi2, s_Wc_hi), (shi2, s_Wc_lo),
                                   (slo2, s_Wc_hi)]:
                        for kk in range(KB):
                            m += 1
                            f = (lambda tb=tb, n=n, kk=kk, at=at, wt=wt, p=p,
                                 st=first, sp=(at is slo2 and kk == KB - 1):
                                 te.matmul(p[:],
                                           at[:, kk, tb * 128:(tb + 1) * 128],
                                           wt[:, kk, n * 512:(n + 1) * 512],
                                           start=st, stop=sp))
                            first = False
                            if m < nmm:
                                E.op(f)
                            else:
                                E.inc(s_mm, f, ("mm3", k, tb))

        def sc_SQ(E, sc, k, which):
            xap = lambda tb: sx[:, k % 3, tb, :]
            acc = s_s2a if which == 1 else s_s2b
            for tb in range(TB):
                if tb:
                    E.op(lambda: sc.drain())
                f = (lambda tb=tb: sc.activation(
                    sLNxm1[:, 0:4, :], xap(tb), AF.Square,
                    accum_out=acc[:, tb:tb + 1]))
                if tb < TB - 1:
                    E.op(f)
                else:
                    E.inc(s_sq, f, ("sq1" if which == 1 else "sq2", k))

        def sc_SQRT(E, sc, k, which):
            vin = s_vara if which == 1 else s_varb
            vout = s_sda if which == 1 else s_sdb
            E.wait(("vv1" if which == 1 else "vv2", k))
            E.inc(s_sq, lambda: sc.activation(vout[:], vin[:], AF.Sqrt),
                  ("sqrt1" if which == 1 else "sqrt2", k))

        def sc_evA(E, sc, k):
            bi = k // NBCH
            for j in range(KB):
                E.wait(("tr1", k, j))
                E.inc(s_b01, lambda j=j, bi=bi: sc.activation(
                    sLNxm1[:, j, :], pb[j % 2][:, 0:CH], AF.Identity,
                    scale=s_ca1[:, bi * 8 + j:bi * 8 + j + 1],
                    bias=s_sa[:, bi * 8 + j:bi * 8 + j + 1]),
                    ("evA", k, j))

        def sc_ev2(E, sc, k):
            bi = k // NBCH
            for j in range(KB):
                E.wait(("tr2", k, j))
                E.inc(s_b01, lambda j=j, bi=bi: sc.activation(
                    sLNxm2[:, j, :], pb[j % 2][:, 0:CH], AF.Identity,
                    scale=s_cb1[:, bi * 8 + j:bi * 8 + j + 1],
                    bias=s_sb[:, bi * 8 + j:bi * 8 + j + 1]),
                    ("ev2", k, j))

        def sc_evU(E, sc, k, i0=0, i1=KB):
            for i in range(i0, i1):
                E.wait(("mm1", k, i))
                if i >= 2:
                    E.wait(("scan", k, i - 2))
                elif k >= 1:
                    E.wait(("scan", k - 1, i + 6))
                E.inc(s_b23, lambda i=i: sc.activation(
                    su[:, i % 2, :], pb[2 + (i % 2)][:, 0:CH], AF.Identity,
                    scale=s_bv[:, i:i + 1], bias=s_bbi[:, i:i + 1]),
                    ("evU", k, i))

        def sc_TANH(E, sc, k):
            for tb in range(TB):
                E.wait(("epiFp", k, tb))
                if tb:
                    E.op(lambda: sc.drain())
                E.inc(s_th, lambda tb=tb: sc.activation(
                    sy_flat[:, tb * D:(tb + 1) * D],
                    sy_flat[:, tb * D:(tb + 1) * D], AF.Tanh),
                      ("tanh", k, tb))

        def stream_sync(E, sy):
            E.eng, E.engname = sy, "sync"
            for k in range(NCH):
                if k >= 3:
                    E.wait(("store", k - 3))
                if k >= 2:
                    E.wait(("xload", k - 2))
                E.inc([sem_x0, sem_x1][k % 2],
                      lambda k=k: sy.dma_start(out=sx[:, k % 3], in_=x_r[k]),
                      ("xload", k), by=16)

        def stream_gp(E, gp):
            E.eng, E.engname = gp, "gpsimd"
            for k in range(NCH):
                E.wait(("out", k, 1))
                if k >= 2:
                    E.wait(("store", k - 2))
                E.inc([st0, st1][k % 2],
                      lambda k=k: gp.dma_start(out=out_r[k], in_=sx[:, k % 3]),
                      ("store", k), by=16)
                if k == NBCH - 1:
                    for q, (slot, dst) in enumerate([(0, s_GA), (1, s_GAB),
                                                     (2, s_GB)]):
                        E.inc(sem_gat,
                              lambda slot=slot, dst=dst: gp.dma_start(
                                  out=dst[:], in_=gascr[slot]),
                              ("gat",) if q == 2 else None, by=16)
            E.wait(("store", NCH - 2))
            E.wait(("store", NCH - 1))

        def stream_T(E, te):
            E.eng, E.engname = te, "tensor"
            t_TR(E, te, "tr1", 0)
            for k in range(NPER):
                if k < NCH:
                    t_MM1(E, te, k)
                if 0 <= k - 1:
                    t_TR(E, te, "tr2", k - 1)
                if k + 1 < NCH:
                    t_TR(E, te, "tr1", k + 1)
                if 0 <= k - 1:
                    t_MM3(E, te, k - 1)
                if k < NCH:
                    t_MM2(E, te, k)

        def stream_V(E, ve):
            E.eng, E.engname = ve, "vector"
            v_LN1(E, ve, 0)
            E.wait(("evA", 0, 7))
            v_split(E, ve, sLNxm1, shi1, slo1, ("spA", 0))
            for k in range(NPER):
                if 0 <= k - 1:
                    v_epiA(E, ve, k - 1)
                if k < NCH:
                    v_scan(E, ve, k, 0, 2)
                if 0 <= k - 1:
                    v_LN2(E, ve, k - 1)
                if k < NCH:
                    v_scan(E, ve, k, 2, 4)
                if k + 1 < NCH:
                    v_LN1(E, ve, k + 1)
                if k < NCH:
                    v_scan(E, ve, k, 4, KB)
                if 0 <= k - 1:
                    E.wait(("ev2", k - 1, 7))
                    E.wait(("mm3", k - 2, 1) if k >= 2 else None)
                    v_split(E, ve, sLNxm2, shi2, slo2, ("sp2", k - 1))
                if k < NCH:
                    E.wait(("mm2", k - 1, 1) if k >= 1 else None)
                    v_split(E, ve, sy_t, shiY, sloY, ("spY", k))
                if k + 1 < NCH:
                    E.wait(("evA", k + 1, 7))
                    E.wait(("mm1", k, 7))
                    v_split(E, ve, sLNxm1, shi1, slo1, ("spA", k + 1))
                if 0 <= k - 1:
                    v_epiF(E, ve, k - 1)

        def stream_SC(E, sc):
            E.eng, E.engname = sc, "scalar"
            E.wait(("xload", 0))
            sc_SQ(E, sc, 0, 1)
            sc_SQRT(E, sc, 0, 1)
            sc_evA(E, sc, 0)
            for k in range(NPER):
                if k < NCH:
                    sc_evU(E, sc, k, 0, 2)
                if k + 1 < NCH:
                    E.wait(("xload", k + 1))
                    E.wait(("vv1", k))
                    E.wait(("spA", k))
                    sc_SQ(E, sc, k + 1, 1)
                if 0 <= k - 1:
                    E.wait(("x2", k - 1))
                    E.wait(("vv2", k - 2) if k >= 2 else None)
                    E.op(lambda: sc.drain())
                    sc_SQ(E, sc, k - 1, 2)
                E.op(lambda: sc.drain())
                E.op(lambda: sc.activation(s_dum[:], s_A[:, 0:1], AF.Sqrt))
                if 0 <= k - 1:
                    sc_SQRT(E, sc, k - 1, 2)
                if k < NCH:
                    sc_evU(E, sc, k, 2, 4)
                if k + 1 < NCH:
                    sc_SQRT(E, sc, k + 1, 1)
                if k < NCH:
                    sc_evU(E, sc, k, 4, KB)
                if 0 <= k - 1:
                    if k < NCH:
                        E.wait(("spA", k))
                    sc_ev2(E, sc, k - 1)
                if k + 1 < NCH:
                    sc_evA(E, sc, k + 1)
                if 0 <= k - 1:
                    sc_TANH(E, sc, k - 1)

        E.plan = True
        stream_sync(E, None)
        stream_gp(E, None)
        stream_T(E, None)
        stream_V(E, None)
        stream_SC(E, None)

        E.plan = False
        EMIT_LOG.append(("__BLOCK__", "bar", "", 0))
        with nc.Block() as blk:
            @blk.sync
            def _(sy):
                stream_sync(E, sy)

            @blk.gpsimd
            def _(gp):
                stream_gp(E, gp)

            @blk.tensor
            def _(te):
                stream_T(E, te)

            @blk.vector
            def _(ve):
                stream_V(E, ve)

            @blk.scalar
            def _(sc):
                stream_SC(E, sc)

        if debug:
            dbg2 = [
                ("sx", sx[:, 0:2], F32), ("sLNx", sLNxA, F32), ("s_s1a", s_s1a, F32),
                ("s_s2a", s_s2a, F32), ("s_vara", s_vara, F32),
                ("s_sda", s_sda, F32), ("s_ra", s_ra, F32), ("s_nma", s_nma, F32),
                ("sLNxm", sLNxm1, F32), ("shiA", shi1, BF16), ("sloA", slo1, BF16),
                ("su", su, F32), ("sy_t", sy_t, F32), ("shiY", shiY, BF16),
                
                ("s_s1b", s_s1b, F32), ("s_varb", s_varb, F32),
                ("s_rb", s_rb, F32), ("scarry", scarry, F32),
            ]
            d2_outs = {}
            from concourse.bass import AP as _APc
            def _ap(t):
                return t if isinstance(t, _APc) else t[:]
            for nm, t, dt_ in dbg2:
                ap = _ap(t)
                d2_outs[nm] = nc.dram_tensor("dbg2_" + nm, list(ap.shape), dt_,
                                             kind="ExternalOutput").ap()
            EMIT_LOG.append(("__BLOCK__", "bar", "", 0))
            with nc.Block() as blk:
                @blk.gpsimd
                def _(gp):
                    n = 0
                    for nm, t, dt_ in dbg2:
                        I(gp.dma_start(out=d2_outs[nm], in_=_ap(t)), "gpsimd", u4, 16)
                        n += 1
                    W(gp, "gpsimd", u4, G["u4"] + 16 * n)
            G["u4"] += 16 * len(dbg2)

    return nc


def kernel(**inputs):
    x = np.ascontiguousarray(np.asarray(inputs["L0_attn_x"], dtype=np.float32))
    w_in = np.asarray(inputs["L0_attn_w_in"], dtype=np.float32)
    w_out = np.asarray(inputs["L0_attn_w_out"], dtype=np.float32)
    wff = np.asarray(inputs["L0_ffn_weight"], dtype=np.float32)
    a1 = np.ascontiguousarray(np.asarray(inputs["L0_ffn_aux_weight1"], dtype=np.float32))
    a2 = np.ascontiguousarray(np.asarray(inputs["L0_ffn_aux_weight2"], dtype=np.float32))
    wmod = np.ascontiguousarray(np.asarray(inputs["L0_wmod"], dtype=np.float32))
    cmat = np.asarray(inputs["L0_c"], dtype=np.float32)

    winT = np.ascontiguousarray(w_in.T)
    woutT = np.ascontiguousarray(w_out.T)
    wT = np.ascontiguousarray(wff.T)
    ident = np.eye(128, dtype=np.float32)
    ones1 = np.ones((1, 128), dtype=np.float32)
    avec = np.ascontiguousarray(np.asarray(inputs["L0_attn_a"], dtype=np.float32))
    bvec = np.ascontiguousarray(np.asarray(inputs["L0_attn_b"], dtype=np.float32))
    binv = np.ascontiguousarray(np.asarray(inputs["L0_attn_b_in"], dtype=np.float32))
    boutv = np.ascontiguousarray(np.asarray(inputs["L0_attn_b_out"], dtype=np.float32))
    fbias = np.ascontiguousarray(np.asarray(inputs["L0_ffn_bias"], dtype=np.float32))

    nc = build_program()
    in_maps = []
    for core in range(8):
        xc = np.ascontiguousarray(x[2 * core:2 * core + 2].reshape(TC, D))
        cT = np.ascontiguousarray(cmat[2 * core:2 * core + 2].T)
        in_maps.append(dict(
            x=xc, winT=winT, woutT=woutT, wT=wT, a1=a1, a2=a2, wmod=wmod,
            cT=cT, avec=avec, bvec=bvec, binv=binv, boutv=boutv, fbias=fbias,
            ident=ident, ones1=ones1,
        ))
    res = run_bass_kernel_spmd(nc, in_maps, core_ids=list(range(8)))
    out = np.concatenate([r["out"] for r in res.results], axis=0)
    return out.reshape(16, 4096, D).astype(np.float32)


# version 19
# speedup vs baseline: 1.0033x; 1.0033x over previous
import contextlib
import numpy as np

import concourse.bass as bass
import concourse.mybir as mybir
from concourse.bass_utils import run_bass_kernel_spmd

F32 = mybir.dt.float32
BF16 = mybir.dt.bfloat16
AF = mybir.ActivationFunctionType
ALU = mybir.AluOpType
AX = mybir.AxisListType

D = 1024
KB = 8
TC = 8192
CH = 256
TB = CH // 128
NCH = TC // CH
NBCH = NCH // 2

EMIT_LOG = []


def build_program(tc=TC, debug=False):
    EMIT_LOG.clear()
    NCH = tc // CH
    NBCH = max(1, NCH // 2)
    nc = bass.Bass("TRN2", target_bir_lowering=False)

    x_h = nc.dram_tensor("x", [tc, D], F32, kind="ExternalInput").ap()
    winT_h = nc.dram_tensor("winT", [D, D], F32, kind="ExternalInput").ap()
    woutT_h = nc.dram_tensor("woutT", [D, D], F32, kind="ExternalInput").ap()
    wT_h = nc.dram_tensor("wT", [D, D], F32, kind="ExternalInput").ap()
    a1_h = nc.dram_tensor("a1", [D, D], F32, kind="ExternalInput").ap()
    a2_h = nc.dram_tensor("a2", [D, D], F32, kind="ExternalInput").ap()
    wmod_h = nc.dram_tensor("wmod", [D, 6 * D], F32, kind="ExternalInput").ap()
    cT_h = nc.dram_tensor("cT", [D, 2], F32, kind="ExternalInput").ap()
    avec_h = nc.dram_tensor("avec", [D], F32, kind="ExternalInput").ap()
    bvec_h = nc.dram_tensor("bvec", [D], F32, kind="ExternalInput").ap()
    bin_h = nc.dram_tensor("binv", [D], F32, kind="ExternalInput").ap()
    bout_h = nc.dram_tensor("boutv", [D], F32, kind="ExternalInput").ap()
    fbias_h = nc.dram_tensor("fbias", [D], F32, kind="ExternalInput").ap()
    ident_h = nc.dram_tensor("ident", [128, 128], F32, kind="ExternalInput").ap()
    ones_h = nc.dram_tensor("ones1", [1, 128], F32, kind="ExternalInput").ap()
    out_h = nc.dram_tensor("out", [tc, D], F32, kind="ExternalOutput").ap()
    t1t_d = nc.dram_tensor("t1t_d", [D, D], F32).ap()
    modscr = nc.dram_tensor("modscr", [2, 6 * D], F32).ap()
    gabscr = nc.dram_tensor("gabscr", [2, D], F32).ap()
    gascr = nc.dram_tensor("gascr", [3, 128, D], F32).ap()

    with contextlib.ExitStack() as stp:
        def sb(name, shape, dt=F32):
            return stp.enter_context(nc.sbuf_tensor(name, shape, dt))

        s_winT_hi = sb("s_winT_hi", [128, KB, D], BF16)
        s_winT_lo = sb("s_winT_lo", [128, KB, D], BF16)
        s_woutT_hi = sb("s_woutT_hi", [128, KB, D], BF16)
        s_woutT_lo = sb("s_woutT_lo", [128, KB, D], BF16)
        s_Wc_hi = sb("s_Wc_hi", [128, KB, D], BF16)
        s_Wc_lo = sb("s_Wc_lo", [128, KB, D], BF16)
        s_id = sb("s_id", [128, 128])
        s_ones = sb("s_ones", [1, 128])
        s_GA = sb("s_GA", [128, D])
        s_GAB = sb("s_GAB", [128, D])
        s_GB = sb("s_GB", [128, D])
        s_BIASC = sb("s_BIASC", [128, D])
        s_ca1 = sb("s_ca1", [128, 16])
        s_sa = sb("s_sa", [128, 16])
        s_cb1 = sb("s_cb1", [128, 16])
        s_sb = sb("s_sb", [128, 16])
        s_A = sb("s_A", [128, KB])
        s_bv = sb("s_bv", [128, KB])
        s_binv = sb("s_binv", [128, KB])
        s_bbi = sb("s_bbi", [128, KB])
        scarry = sb("scarry", [128, KB])
        s_cT = sb("s_cT", [128, KB, 2])
        s_silc = sb("s_silc", [128, KB, 2])
        s_av = sb("s_av", [128, KB])
        s_s1a = sb("s_s1a", [128, TB]); s_s2a = sb("s_s2a", [128, TB])
        s_s1b = sb("s_s1b", [128, TB]); s_s2b = sb("s_s2b", [128, TB])
        s_vara = sb("s_vara", [128, TB]); s_varb = sb("s_varb", [128, TB])
        s_sda = sb("s_sda", [128, TB]); s_sdb = sb("s_sdb", [128, TB])
        s_ra = sb("s_ra", [128, TB]); s_rb = sb("s_rb", [128, TB])
        s_nma = sb("s_nma", [128, TB]); s_nmb = sb("s_nmb", [128, TB])
        s_q2 = sb("s_q2", [128, TB])

        pb = [stp.enter_context(nc.psum_tensor(f"pb{i}", [128, 512], F32))
              for i in range(8)]

        SEMS = {}
        NAME = {}
        def sem(name):
            s = stp.enter_context(nc.semaphore(name))
            SEMS[name] = s
            NAME[id(s)] = name
            return s

        sem_x = sem("sem_x"); sem_st = sem("sem_st"); sem_gat = sem("sem_gat")
        s_v = sem("s_v"); s_vv = sem("s_vv"); s_sq = sem("s_sq")
        s_tr = sem("s_tr"); s_b01 = sem("s_b01"); s_sp = sem("s_sp")
        s_mm = sem("s_mm"); s_b23 = sem("s_b23"); s_b47 = sem("s_b47")
        s_th = sem("s_th"); s_out = sem("s_out")
        s_scan = sem("s_scan")
        u1 = sem("u1"); u2 = sem("u2"); u3 = sem("u3"); u4 = sem("u4")
        d0 = sem("d0"); d1 = sem("d1")
        sem_x0 = sem("sem_x0"); sem_x1 = sem("sem_x1")
        st0 = sem("st0"); st1 = sem("st1")

        def W(eng, engname, smp, val):
            EMIT_LOG.append((engname, "w", NAME[id(smp)], val))
            if val > 0:
                eng.wait_ge(smp, val)

        def I(ins, engname, smp, val=1):
            EMIT_LOG.append((engname, "i", NAME[id(smp)], val))
            return ins.then_inc(smp, val)

        G = {"u1": 0, "u2": 0, "u3": 0, "u4": 0}

        with contextlib.ExitStack() as st2:
            def sb2(name, shape, dt=F32):
                return st2.enter_context(nc.sbuf_tensor(name, shape, dt))
            B1 = sb2("B1", [128, KB, D])
            B2h = sb2("B2h", [128, KB, D], BF16)
            B2l = sb2("B2l", [128, KB, D], BF16)
            B3h = s_woutT_hi
            B3l = s_woutT_lo
            WCSh = s_winT_hi
            WCSl = s_winT_lo
            s_rowm = sb2("s_rowm", [2, 512])
            u5 = sem("u5")
            u6 = sem("u6")

            wmod_r = wmod_h.rearrange("(k p) m -> p k m", p=128)
            a1_r = a1_h.rearrange("(k p) c -> p k c", p=128)
            wt_r = wT_h.rearrange("(k p) c -> p k c", p=128)
            a2_r = a2_h.rearrange("(k p) c -> p k c", p=128)
            T1T_UNITS = ([(j, 0) for j in range(8)]
                         + [(j, 1) for j in range(8)])
            AVAIL = [2] * 4 + [5] * 4 + [7] * 8
            V_ORDER = (["suba1h0", "subwth0", "ev0", "ev1", "suba1h1", "ev2",
                        "subwth1"] + [f"ev{p}" for p in range(3, 16)] + ["a2sub"]
                       + [f"wce{m}" for m in range(16)] + ["cph", "cpl"]
                       + [f"me{n}" for n in range(12)])
            V2 = {nm: i + 1 for i, nm in enumerate(V_ORDER)}
            EMIT_LOG.append(("__BLOCK__", "bar", "", 0))
            with nc.Block() as blk:
                @blk.sync
                def _(sy):
                    I(sy.dma_start(out=B1[:, :, 0:512], in_=a1_r[:, :, 0:512]),
                      "sync", u6, 16)
                    small = [
                        (s_id[:], ident_h),
                        (s_ones[:], ones_h),
                        (s_cT[:], cT_h.rearrange("(k p) b -> p k b", p=128)),
                        (s_av[:], avec_h.rearrange("(k p) -> p k", p=128)),
                        (s_bv[:], bvec_h.rearrange("(k p) -> p k", p=128)),
                        (s_binv[:], bin_h.rearrange("(k p) -> p k", p=128)),
                    ]
                    with nc.allow_non_contiguous_dma(reason="small setup vectors"):
                        for dst, srcap in small:
                            I(sy.dma_start(out=dst, in_=srcap), "sync", u1, 16)
                    W(sy, "sync", u2, V2["suba1h0"])
                    I(sy.dma_start(out=B1[:, :, 0:512], in_=a1_r[:, :, 512:1024]),
                      "sync", u6, 16)
                    W(sy, "sync", u2, V2["subwth1"])
                    I(sy.dma_start(out=B1[:], in_=a2_r), "sync", u6, 16)
                    for n in range(0, 12, 2):
                        W(sy, "sync",
                          u2 if n == 0 else u3, V2["a2sub"] if n == 0 else 33 + n - 2)
                        I(sy.dma_start(
                            out=B1[:, :, 0:512],
                            in_=wmod_r[:, :, n * 512:(n + 1) * 512]), "sync",
                          d0, 16)

                @blk.scalar
                def _(sc):
                    W(sc, "scalar", u6, 16)
                    I(sc.activation(B2h[:, :, 0:512], B1[:, :, 0:512],
                                    AF.Identity), "scalar", u5)
                    W(sc, "scalar", u4, 16)
                    I(sc.activation(B3h[:, :, 0:512], B1[:, :, 512:1024],
                                    AF.Identity), "scalar", u5)
                    W(sc, "scalar", u1, 16 * 6)
                    sc.activation(s_silc[:], s_cT[:], AF.Silu)
                    I(sc.activation(s_A[:], s_av[:], AF.Sigmoid), "scalar", u5)
                    W(sc, "scalar", u6, 32)
                    I(sc.activation(B2h[:, :, 512:1024], B1[:, :, 0:512],
                                    AF.Identity), "scalar", u5)
                    W(sc, "scalar", u4, 32)
                    I(sc.activation(B3h[:, :, 512:1024], B1[:, :, 512:1024],
                                    AF.Identity), "scalar", u5)
                    W(sc, "scalar", u6, 48)
                    W(sc, "scalar", u3, 16)
                    I(sc.activation(B2h[:], B1[:], AF.Identity), "scalar", u5)

                @blk.vector
                def _(ve):
                    for nm in V_ORDER:
                        if nm == "suba1h0":
                            W(ve, "vector", u5, 1)
                            I(ve.tensor_tensor(B2l[:, :, 0:512], B1[:, :, 0:512],
                                               B2h[:, :, 0:512], ALU.subtract),
                              "vector", u2)
                        elif nm == "subwth0":
                            W(ve, "vector", u5, 2)
                            I(ve.tensor_tensor(B3l[:, :, 0:512],
                                               B1[:, :, 512:1024],
                                               B3h[:, :, 0:512], ALU.subtract),
                              "vector", u2)
                            W(ve, "vector", u1, 16 * 6)
                            ve.tensor_tensor(s_bbi[:], s_bv[:], s_binv[:],
                                             ALU.mult)
                        elif nm == "suba1h1":
                            W(ve, "vector", u5, 4)
                            I(ve.tensor_tensor(B2l[:, :, 512:1024],
                                               B1[:, :, 0:512],
                                               B2h[:, :, 512:1024], ALU.subtract),
                              "vector", u2)
                        elif nm == "subwth1":
                            W(ve, "vector", u5, 5)
                            I(ve.tensor_tensor(B3l[:, :, 512:1024],
                                               B1[:, :, 512:1024],
                                               B3h[:, :, 512:1024], ALU.subtract),
                              "vector", u2)
                        elif nm == "a2sub":
                            W(ve, "vector", u5, 6)
                            I(ve.tensor_tensor(B2l[:], B1[:], B2h[:],
                                               ALU.subtract), "vector", u2)
                        elif nm == "cph":
                            W(ve, "vector", u3, 32)
                            ve.drain()
                            I(ve.tensor_copy(s_Wc_hi[:], WCSh[:]), "vector", u2)
                        elif nm == "cpl":
                            I(ve.tensor_copy(s_Wc_lo[:], WCSl[:]), "vector", u2)
                        elif nm.startswith("ev"):
                            p = int(nm[2:])
                            j, n = T1T_UNITS[p]
                            W(ve, "vector", u3, p + 1)
                            hi = s_Wc_hi[:, j, n * 512:(n + 1) * 512]
                            lo = s_Wc_lo[:, j, n * 512:(n + 1) * 512]
                            ve.tensor_copy(hi, pb[p % 2][:])
                            ve.drain()
                            I(ve.tensor_tensor(lo, pb[p % 2][:], hi,
                                               ALU.subtract), "vector", u2)
                        elif nm.startswith("wce"):
                            m = int(nm[3:])
                            j, n = m // 2, m % 2
                            W(ve, "vector", u3, 16 + m + 1)
                            hi = WCSh[:, j, n * 512:(n + 1) * 512]
                            lo = WCSl[:, j, n * 512:(n + 1) * 512]
                            ve.tensor_copy(hi, pb[m % 2][:])
                            ve.drain()
                            I(ve.tensor_tensor(lo, pb[m % 2][:], hi,
                                               ALU.subtract), "vector", u2)
                        else:
                            n = int(nm[2:])
                            W(ve, "vector", u3, 32 + n + 1)
                            if n >= 1:
                                W(ve, "vector", u4, 32 + 16 * n)
                            I(ve.tensor_copy(s_rowm[:], pb[n % 2][0:2, :]),
                              "vector", u2)

                @blk.tensor
                def _(te):
                    for p, (j, n) in enumerate(T1T_UNITS):
                        w = AVAIL[p]
                        if p >= 2:
                            w = max(w, V2[f"ev{p-2}"])
                        W(te, "tensor", u2, w)
                        last = None
                        first = True
                        for lh, rh in [(B2h, B3h), (B2h, B3l), (B2l, B3h)]:
                            for k in range(KB):
                                last = te.matmul(
                                    pb[p % 2][:], lh[:, k, j * 128:(j + 1) * 128],
                                    rh[:, k, n * 512:(n + 1) * 512],
                                    start=first, stop=(lh is B2l and k == KB - 1))
                                first = False
                        I(last, "tensor", u3)
                    for m in range(16):
                        j, n = m // 2, m % 2
                        w = V2["a2sub"]
                        if m >= 2:
                            w = max(w, V2[f"wce{m-2}"])
                        W(te, "tensor", u2, w)
                        last = None
                        first = True
                        for lh, rh in [(s_Wc_hi, B2h), (s_Wc_hi, B2l),
                                       (s_Wc_lo, B2h)]:
                            for k in range(KB):
                                last = te.matmul(
                                    pb[m % 2][:], lh[:, k, j * 128:(j + 1) * 128],
                                    rh[:, k, n * 512:(n + 1) * 512],
                                    start=first, stop=(lh is s_Wc_lo and k == KB - 1))
                                first = False
                        I(last, "tensor", u3)
                    W(te, "tensor", u5, 3)
                    for n in range(12):
                        W(te, "tensor", [d0, d1][n % 2], 16 * (n // 2 + 1))
                        W(te, "tensor", u2,
                          V2[f"me{n-2}"] if n >= 2 else V2[f"wce{14+n}"])
                        last = None
                        for k in range(KB):
                            last = te.matmul(
                                pb[n % 2][0:2, :], s_silc[:, k, :],
                                B1[:, k, (n % 2) * 512:(n % 2) * 512 + 512],
                                start=(k == 0), stop=(k == KB - 1))
                        I(last, "tensor", u3)

                @blk.gpsimd
                def _(gp):
                    I(gp.dma_start(out=B1[:, :, 512:1024], in_=wt_r[:, :, 0:512]),
                      "gpsimd", u4, 16)
                    W(gp, "gpsimd", u2, V2["subwth0"])
                    I(gp.dma_start(out=B1[:, :, 512:1024],
                                   in_=wt_r[:, :, 512:1024]), "gpsimd", u4, 16)
                    gp_ops = ["c1", "c3", "s0", "c5", "s1", "s2", "c7", "s3",
                              "s4", "c9", "s5", "s6", "c11", "s7", "s8", "s9",
                              "s10", "s11"]
                    for nm in gp_ops:
                        n = int(nm[1:])
                        if nm[0] == "c":
                            W(gp, "gpsimd",
                              u2 if n == 1 else u3,
                              V2["a2sub"] if n == 1 else 33 + n - 2)
                            I(gp.dma_start(
                                out=B1[:, :, 512:1024],
                                in_=wmod_r[:, :, n * 512:(n + 1) * 512]),
                              "gpsimd", d1, 16)
                        else:
                            W(gp, "gpsimd", u2, V2[f"me{n}"])
                            I(gp.dma_start(out=modscr[:, n * 512:(n + 1) * 512],
                                           in_=s_rowm[:]), "gpsimd", u4, 16)
                    W(gp, "gpsimd", u4, 16 * 14)
            G["u1"] = 16 * 6; G["u2"] = len(V_ORDER); G["u3"] = 44
            G["u4"] = 16 * 14

        with contextlib.ExitStack() as st2:
            def sb2(name, shape, dt=F32):
                return st2.enter_context(nc.sbuf_tensor(name, shape, dt))
            B4 = sb2("B4", [128, KB, D])
            s_rga = sb2("s_rga", [2, D])
            s_rgb = sb2("s_rgb", [2, D])
            s_bout2 = sb2("s_bout2", [2, D])
            s_row2 = sb2("s_row2", [2, D])
            s_mask = sb2("s_mask", [2, D])
            s_sgn = sb2("s_sgn", [2, D])
            s_row1 = s_row2[0:1, :]
            s_rgab = s_sgn

            B4b = sb2("B4b", [128, KB, D])
            s_rowB = sb2("s_rowB", [1, D])
            bc_jobs = [
                (s_GA, 0, modscr[1, 2 * D:3 * D]),
                (s_GAB, 1, gabscr[1, :]),
                (s_GB, 2, modscr[1, 5 * D:6 * D]),
                (s_GA, None, modscr[0, 2 * D:3 * D]),
                (s_GAB, None, gabscr[0, :]),
                (s_GB, None, modscr[0, 5 * D:6 * D]),
                (s_BIASC, None, fbias_h),
            ]
            b1 = G["u1"]; b2 = G["u2"]; b3 = G["u3"]; b4 = G["u4"]; b5 = 6
            u7 = sem("u7")
            winT_r = winT_h.rearrange("(k p) e -> p k e", p=128)
            woutT_r = woutT_h.rearrange("(k p) e -> p k e", p=128)
            EMIT_LOG.append(("__BLOCK__", "bar", "", 0))
            with nc.Block() as blk:
                @blk.sync
                def _(sy):
                    one = lambda ap: ap.rearrange("(a d) -> a d", a=1)
                    I(sy.dma_start(out=s_rga[:], in_=modscr[:, 2 * D:3 * D]),
                      "sync", u1, 16)
                    I(sy.dma_start(out=s_rgb[:], in_=modscr[:, 5 * D:6 * D]),
                      "sync", u1, 16)
                    I(sy.dma_start(out=s_bout2[0:1, :], in_=one(bout_h)),
                      "sync", u1, 16)
                    I(sy.dma_start(out=s_bout2[1:2, :], in_=one(bout_h)),
                      "sync", u1, 16)
                    W(sy, "sync", u4, b4 + 64)
                    for q in range(7):
                        W(sy, "sync", u1, b1 + 64 + 16 * q)
                        if q >= 2:
                            W(sy, "sync", u3, b3 + 2 * (q - 1))
                        I(sy.dma_start(out=[s_row1, s_rowB[:]][q % 2],
                                       in_=one(bc_jobs[q][2])), "sync", u1, 16)
                    W(sy, "sync", u1, b1 + 176)
                    I(sy.dma_start(out=B4[:, :, 0:512],
                                   in_=winT_r[:, :, 0:512]), "sync", u1, 16)
                    W(sy, "sync", u1, b1 + 192)
                    I(sy.dma_start(out=B4[:, :, 512:1024],
                                   in_=winT_r[:, :, 512:1024]), "sync", u1, 16)
                    with nc.allow_non_contiguous_dma(reason="small mod vectors"):
                        for bi in range(2):
                            for dst, off in [(s_sa, 0), (s_ca1, 1),
                                             (s_sb, 3), (s_cb1, 4)]:
                                srcv = modscr[bi, off * D:(off + 1) * D]
                                I(sy.dma_start(
                                    out=dst[:, bi * 8:(bi + 1) * 8],
                                    in_=srcv.rearrange("(k p) -> p k", p=128)),
                                  "sync", u7, 16)

                @blk.gpsimd
                def _(gp):
                    I(gp.dma_start(out=B4b[:, :, 0:512],
                                   in_=woutT_r[:, :, 0:512]), "gpsimd", u4, 16)
                    W(gp, "gpsimd", u2, b2 + 1)
                    W(gp, "gpsimd", u4, b4 + 16)
                    I(gp.dma_start(out=modscr[:, 2 * D:3 * D], in_=s_rga[:]),
                      "gpsimd", u4, 16)
                    W(gp, "gpsimd", u4, b4 + 32)
                    I(gp.dma_start(out=modscr[:, 5 * D:6 * D], in_=s_rgb[:]),
                      "gpsimd", u4, 16)
                    W(gp, "gpsimd", u4, b4 + 48)
                    I(gp.dma_start(out=gabscr[:], in_=s_rgab[:]), "gpsimd",
                      u4, 16)
                    for q2, (slot, dst) in enumerate([(0, s_GA), (1, s_GAB),
                                                      (2, s_GB)]):
                        W(gp, "gpsimd", u5, b5 + 2 * q2 + 2)
                        W(gp, "gpsimd", u4, b4 + 64 + 16 * q2)
                        I(gp.dma_start(out=gascr[slot], in_=dst[:]), "gpsimd",
                          u4, 16)
                    W(gp, "gpsimd", u4, b4 + 112)
                    I(gp.dma_start(out=B4b[:, :, 512:1024],
                                   in_=woutT_r[:, :, 512:1024]), "gpsimd", u4, 16)
                    W(gp, "gpsimd", u4, b4 + 128)

                @blk.tensor
                def _(te):
                    for q in range(7):
                        W(te, "tensor", u1, b1 + 64 + 16 * (q + 1))
                        for n in range(2):
                            m = 2 * q + n
                            if m >= 2:
                                W(te, "tensor", u5, b5 + m - 1)
                            rb = [s_row1, s_rowB[:]][q % 2]
                            I(te.matmul(
                                pb[m % 2][:], s_ones[:],
                                rb[:, n * 512:(n + 1) * 512],
                                start=True, stop=True), "tensor", u3)

                @blk.scalar
                def _(sc):
                    for m in range(14):
                        q, n = m // 2, m % 2
                        W(sc, "scalar", u3, b3 + m + 1)
                        if m in (6, 8, 10):
                            W(sc, "scalar", u4, b4 + 80 + 16 * ((m - 6) // 2))
                        dst = bc_jobs[q][0]
                        I(sc.activation(dst[:, n * 512:(n + 1) * 512],
                                        pb[m % 2][:], AF.Identity), "scalar", u5)
                    W(sc, "scalar", u4, b4 + 16)
                    I(sc.activation(s_woutT_hi[:, :, 0:512], B4b[:, :, 0:512],
                                    AF.Identity), "scalar", u5)
                    W(sc, "scalar", u1, b1 + 192)
                    I(sc.activation(s_winT_hi[:, :, 0:512], B4[:, :, 0:512],
                                    AF.Identity), "scalar", u5)
                    W(sc, "scalar", u4, b4 + 128)
                    I(sc.activation(s_woutT_hi[:, :, 512:1024],
                                    B4b[:, :, 512:1024], AF.Identity),
                      "scalar", u5)
                    W(sc, "scalar", u1, b1 + 208)
                    I(sc.activation(s_winT_hi[:, :, 512:1024],
                                    B4[:, :, 512:1024], AF.Identity),
                      "scalar", u5)

                @blk.vector
                def _(ve):
                    W(ve, "vector", u1, b1 + 64)
                    for rr in (s_rga, s_rgb):
                        ve.tensor_single_scalar(s_mask[:], rr[:], 0.0, ALU.is_ge)
                        ve.drain()
                        ve.tensor_scalar(s_sgn[:], s_mask[:], 2.0, -1.0,
                                         ALU.mult, ALU.add)
                        ve.drain()
                        ve.tensor_tensor(s_mask[:], rr[:], s_sgn[:], ALU.mult)
                        ve.drain()
                        ve.tensor_scalar_max(s_mask[:], s_mask[:], 0.25)
                        ve.drain()
                        ve.tensor_tensor(rr[:], s_mask[:], s_sgn[:], ALU.mult)
                        ve.drain()
                    I(ve.tensor_tensor(s_rgab[:], s_rga[:], s_bout2[:],
                                       ALU.mult), "vector", u2)
                    for h, (lo, src, hi, wv) in enumerate([
                            (s_woutT_lo, B4b, s_woutT_hi, b5 + 15),
                            (s_winT_lo, B4, s_winT_hi, b5 + 16),
                            (s_woutT_lo, B4b, s_woutT_hi, b5 + 17),
                            (s_winT_lo, B4, s_winT_hi, b5 + 18)]):
                        c0 = (h // 2) * 512
                        W(ve, "vector", u5, wv)
                        I(ve.tensor_tensor(lo[:, :, c0:c0 + 512],
                                           src[:, :, c0:c0 + 512],
                                           hi[:, :, c0:c0 + 512], ALU.subtract),
                          "vector", u2)
                    W(ve, "vector", u7, 16 * 8)
                    ve.tensor_scalar_add(s_ca1[:], s_ca1[:], 1.0)
                    ve.drain()
                    ve.tensor_scalar_add(s_cb1[:], s_cb1[:], 1.0)
            G["u1"] += 208; G["u2"] += 5; G["u3"] += 14; G["u4"] += 128

        if debug:
            dbg_specs = [
                ("s_A", s_A, F32), ("s_bbi", s_bbi, F32), ("s_silc", s_silc, F32),
                ("s_ca1", s_ca1, F32), ("s_sa", s_sa, F32),
                ("s_cb1", s_cb1, F32), ("s_sb", s_sb, F32),
                ("s_GA", s_GA, F32), ("s_GAB", s_GAB, F32), ("s_GB", s_GB, F32),
                ("s_BIASC", s_BIASC, F32),
                ("s_winT_hi", s_winT_hi, BF16), ("s_winT_lo", s_winT_lo, BF16),
                ("s_woutT_hi", s_woutT_hi, BF16),
                ("s_Wc_hi", s_Wc_hi, BF16), ("s_Wc_lo", s_Wc_lo, BF16),
                ("s_id", s_id, F32), ("s_ones", s_ones, F32),
            ]
            dbg_outs = {}
            for nm, t, dt_ in dbg_specs:
                shp = list(t.shape) if hasattr(t, "shape") else None
                ap = t[:]
                dbg_outs[nm] = nc.dram_tensor("dbg_" + nm, list(ap.shape), dt_,
                                              kind="ExternalOutput").ap()
            EMIT_LOG.append(("__BLOCK__", "bar", "", 0))
            with nc.Block() as blk:
                @blk.gpsimd
                def _(gp):
                    n = 0
                    for nm, t, dt_ in dbg_specs:
                        I(gp.dma_start(out=dbg_outs[nm], in_=t[:]), "gpsimd", u4, 16)
                        n += 1
                    W(gp, "gpsimd", u4, G["u4"] + 16 * n)
            G["u4"] += 16 * len(dbg_specs)

        sx = sb("sx", [128, 3, TB, D])
        sLNxA = sb("sLNxA", [128, TB, D])
        sLNxB = sb("sLNxB", [128, TB, D])
        sLNxm1 = sb("sLNxm1", [128, KB, CH])
        sLNxm2 = sb("sLNxm2", [128, KB, CH])
        shi1 = sb("shi1", [128, KB, CH], BF16)
        slo1 = sb("slo1", [128, KB, CH], BF16)
        shi2 = sb("shi2", [128, KB, CH], BF16)
        slo2 = sb("slo2", [128, KB, CH], BF16)
        su = sb("su", [128, 2, CH])
        stmp = sb("stmp", [128, 512])
        s_dum = sb("s_dum", [128, 1])
        sy_t = sb("sy_t", [128, KB, CH])
        shiY = sb("shiY", [128, KB, CH], BF16)
        sloY = sb("sloY", [128, KB, CH], BF16)

        x_r = x_h.rearrange("(c tb p) d -> c p tb d", p=128, tb=TB)
        out_r = out_h.rearrange("(c tb p) d -> c p tb d", p=128, tb=TB)

        sy_flat = sy_t[:].rearrange("p a b -> p (a b)")

        class EV:
            def __init__(self):
                self.counts = {}
                self.events = {}
                self.plan = True
                self.eng = None
                self.engname = ""

            def op(self, f):
                if not self.plan:
                    f()

            def inc(self, sem, f, key=None, by=1):
                nm = NAME[id(sem)]
                if self.plan:
                    v = self.counts.get(nm, 0) + by
                    self.counts[nm] = v
                    if key is not None:
                        assert key not in self.events, key
                        self.events[key] = (sem, v)
                else:
                    EMIT_LOG.append((self.engname, "i", nm, by))
                    f().then_inc(sem, by)

            def wait(self, key):
                if key is None or self.plan:
                    return
                sem, v = self.events[key]
                EMIT_LOG.append((self.engname, "w", NAME[id(sem)], v))
                self.eng.wait_ge(sem, v)

        E = EV()
        NPER = NCH + 1

        seq01 = [("tr1", 0)]
        for k in range(NPER):
            if 0 <= k - 1 < NCH:
                seq01.append(("tr2", k - 1))
            if k + 1 < NCH:
                seq01.append(("tr1", k + 1))
        EVN = {"tr1": "evA", "tr2": "ev2"}
        prev01 = {}
        lastb = {0: None, 1: None}
        for kind, kk in seq01:
            for j in range(KB):
                prev01[(kind, kk, j)] = lastb[j % 2]
                lastb[j % 2] = (EVN[kind], kk, j)

        def prev23(k, i):
            if i >= 2:
                return ("evU", k, i - 2)
            if k >= 1:
                return ("evU", k - 1, 6 + i)
            return None

        def v_LN1(E, ve, k):
            E.wait(("xload", k))
            for tb in range(TB):
                E.op(lambda tb=tb: ve.reduce_sum(
                    s_s1a[:, tb:tb + 1], sx[:, k % 3, tb, :], axis=AX.X))
            E.op(lambda: ve.drain())
            E.wait(("sq1", k))
            E.op(lambda: ve.tensor_scalar_mul(s_nma[:], s_s1a[:], 1.0 / D))
            E.op(lambda: ve.tensor_scalar(s_vara[:], s_s2a[:], 1.0 / D, 1e-5,
                                          ALU.mult, ALU.add))
            E.op(lambda: ve.drain())
            E.op(lambda: ve.tensor_tensor(s_q2[:], s_nma[:], s_nma[:], ALU.mult))
            E.op(lambda: ve.drain())
            E.inc(s_vv, lambda: ve.tensor_tensor(s_vara[:], s_vara[:], s_q2[:],
                                                 ALU.subtract), ("vv1", k))
            E.wait(("sqrt1", k))
            E.op(lambda: ve.reciprocal(s_ra[:], s_sda[:]))
            E.op(lambda: ve.drain())
            E.op(lambda: ve.scalar_tensor_tensor(s_nma[:], s_nma[:], -1.0, s_ra[:],
                                                 ALU.mult, ALU.mult))
            E.op(lambda: ve.drain())
            if k >= 1:
                E.wait(("tr1", k - 1, 7))
            for tb in range(TB):
                f = (lambda tb=tb: ve.tensor_scalar(
                    sLNxB[:, tb, :], sx[:, k % 3, tb, :],
                    s_ra[:, tb:tb + 1], s_nma[:, tb:tb + 1], ALU.mult, ALU.add))
                if tb < TB - 1:
                    E.op(f)
                else:
                    E.inc(s_v, f, ("c1", k))

        def v_LN2(E, ve, k):
            for tb in range(TB):
                E.op(lambda tb=tb: ve.reduce_sum(
                    s_s1b[:, tb:tb + 1], sx[:, k % 3, tb, :], axis=AX.X))
            E.op(lambda: ve.drain())
            E.wait(("sq2", k))
            E.op(lambda: ve.tensor_scalar_mul(s_nmb[:], s_s1b[:], 1.0 / D))
            E.op(lambda: ve.tensor_scalar(s_varb[:], s_s2b[:], 1.0 / D, 1e-5,
                                          ALU.mult, ALU.add))
            E.op(lambda: ve.drain())
            E.op(lambda: ve.tensor_tensor(s_q2[:], s_nmb[:], s_nmb[:], ALU.mult))
            E.op(lambda: ve.drain())
            E.inc(s_vv, lambda: ve.tensor_tensor(s_varb[:], s_varb[:], s_q2[:],
                                                 ALU.subtract), ("vv2", k))
            E.wait(("sqrt2", k))
            E.op(lambda: ve.reciprocal(s_rb[:], s_sdb[:]))
            E.op(lambda: ve.drain())
            E.op(lambda: ve.scalar_tensor_tensor(s_nmb[:], s_nmb[:], -1.0, s_rb[:],
                                                 ALU.mult, ALU.mult))
            E.op(lambda: ve.drain())
            if k >= 1:
                E.wait(("tr2", k - 1, 7))
            for tb in range(TB):
                f = (lambda tb=tb: ve.tensor_scalar(
                    sLNxA[:, tb, :], sx[:, k % 3, tb, :],
                    s_rb[:, tb:tb + 1], s_nmb[:, tb:tb + 1], ALU.mult, ALU.add))
                if tb < TB - 1:
                    E.op(f)
                else:
                    E.inc(s_v, f, ("c2", k))

        def v_split(E, ve, src, hi, lo, key):
            E.op(lambda: ve.tensor_copy(hi[:], src[:]))
            E.op(lambda: ve.drain())
            E.inc(s_sp, lambda: ve.tensor_tensor(lo[:], src[:], hi[:],
                                                 ALU.subtract), key)

        def v_scan(E, ve, k, i0, i1):
            if i0 == 0 and k % NBCH == 0:
                E.op(lambda: ve.memset(scarry[:], 0.0))
                E.op(lambda: ve.drain())
            for i in range(i0, i1):
                E.wait(("evU", k, i))
                E.inc(s_scan, lambda i=i: ve.tensor_tensor_scan(
                    sy_t[:, i, :], s_A[:, i:i + 1].broadcast_to([128, CH]),
                    su[:, i % 2, :], scarry[:, i:i + 1], ALU.mult, ALU.add),
                    ("scan", k, i))
            if i1 == KB:
                E.op(lambda: ve.drain())
                for i in range(KB):
                    E.op(lambda i=i: ve.tensor_copy(
                        scarry[:, i:i + 1], sy_t[:, i, CH - 1:CH]))
                E.op(lambda: ve.drain())

        def v_epiA(E, ve, k):
            if k == NBCH:
                E.wait(("gat",))
            for tb in range(TB):
                E.wait(("mm2", k, tb))
                for n in range(2):
                    E.inc(s_b47, lambda tb=tb, n=n: ve.tensor_tensor(
                        stmp[:],
                        pb[4 + 2 * tb + n][:],
                        s_GA[:, n * 512:(n + 1) * 512], ALU.mult),
                        ("epiAp", k, tb) if n == 1 else None)
                    E.op(lambda: ve.drain())
                    E.op(lambda tb=tb, n=n: ve.tensor_tensor(
                        sx[:, k % 3, tb, n * 512:(n + 1) * 512], stmp[:],
                        sx[:, k % 3, tb, n * 512:(n + 1) * 512], ALU.add))
                    E.op(lambda: ve.drain())
                f = (lambda tb=tb: ve.tensor_tensor(
                    sx[:, k % 3, tb, :], sx[:, k % 3, tb, :], s_GAB[:], ALU.add))
                if tb < TB - 1:
                    E.op(f)
                else:
                    E.inc(s_v, f, ("x2", k))
            E.op(lambda: ve.drain())

        def v_epiF(E, ve, k):
            E.op(lambda: ve.drain())
            for tb in range(TB):
                E.wait(("mm3", k, tb))
                for n in range(2):
                    E.inc(s_b47, lambda tb=tb, n=n: ve.tensor_tensor(
                        sy_flat[:, tb * D + n * 512:tb * D + (n + 1) * 512],
                        pb[4 + 2 * tb + n][:],
                        s_BIASC[:, n * 512:(n + 1) * 512], ALU.add),
                        ("epiFp", k, tb) if n == 1 else None)
            E.wait(("sq2", k))
            for tb in range(TB):
                E.wait(("tanh", k, tb))
                E.op(lambda tb=tb: ve.tensor_tensor(
                    sy_flat[:, tb * D:(tb + 1) * D],
                    sy_flat[:, tb * D:(tb + 1) * D], s_GB[:], ALU.mult))
                E.op(lambda: ve.drain())
                E.inc(s_out, lambda tb=tb: ve.tensor_tensor(
                    sx[:, k % 3, tb, :], sy_flat[:, tb * D:(tb + 1) * D],
                    sx[:, k % 3, tb, :], ALU.add),
                    ("out", k, tb))

        def t_TR(E, te, kind, k):
            E.wait(("c1" if kind == "tr1" else "c2", k))
            buf = sLNxB if kind == "tr1" else sLNxA
            for j in range(KB):
                E.wait(prev01[(kind, k, j)])
                for tb in range(TB):
                    f = (lambda j=j, tb=tb, buf=buf: te.transpose(
                        pb[j % 2][:, tb * 128:(tb + 1) * 128],
                        buf[:, tb, j * 128:(j + 1) * 128], s_id[:]))
                    if tb < TB - 1:
                        E.op(f)
                    else:
                        E.inc(s_tr, f, (kind, k, j))

        def t_MM1(E, te, k):
            E.wait(("spA", k))
            for i in range(KB):
                E.wait(prev23(k, i))
                p = pb[2 + (i % 2)]
                nmm = 3 * KB
                m = 0
                for wt, at in [(s_winT_hi, shi1), (s_winT_hi, slo1),
                               (s_winT_lo, shi1)]:
                    for kk in range(KB):
                        m += 1
                        f = (lambda i=i, kk=kk, wt=wt, at=at, p=p,
                             st=(m == 1), sp=(m == nmm): te.matmul(
                            p[:, 0:CH], wt[:, kk, i * 128:(i + 1) * 128],
                            at[:, kk, :], start=st, stop=sp))
                        if m < nmm:
                            E.op(f)
                        else:
                            E.inc(s_mm, f, ("mm1", k, i))

        def t_MM2(E, te, k):
            E.wait(("spY", k))
            for tb in range(TB):
                E.wait(("epiFp", k - 1, tb) if k >= 1 else None)
                m = 0
                nmm = 2 * 3 * KB
                for n in range(2):
                    p = pb[4 + 2 * tb + n]
                    first = True
                    for yt, wt in [(shiY, s_woutT_hi), (shiY, s_woutT_lo),
                                   (sloY, s_woutT_hi)]:
                        for kk in range(KB):
                            m += 1
                            f = (lambda tb=tb, n=n, kk=kk, yt=yt, wt=wt, p=p,
                                 st=first, sp=(yt is sloY and kk == KB - 1):
                                 te.matmul(p[:],
                                           yt[:, kk, tb * 128:(tb + 1) * 128],
                                           wt[:, kk, n * 512:(n + 1) * 512],
                                           start=st, stop=sp))
                            first = False
                            if m < nmm:
                                E.op(f)
                            else:
                                E.inc(s_mm, f, ("mm2", k, tb))

        def t_MM3(E, te, k):
            E.wait(("sp2", k))
            for tb in range(TB):
                E.wait(("epiAp", k, tb))
                m = 0
                nmm = 2 * 3 * KB
                for n in range(2):
                    p = pb[4 + 2 * tb + n]
                    first = True
                    for at, wt in [(shi2, s_Wc_hi), (shi2, s_Wc_lo),
                                   (slo2, s_Wc_hi)]:
                        for kk in range(KB):
                            m += 1
                            f = (lambda tb=tb, n=n, kk=kk, at=at, wt=wt, p=p,
                                 st=first, sp=(at is slo2 and kk == KB - 1):
                                 te.matmul(p[:],
                                           at[:, kk, tb * 128:(tb + 1) * 128],
                                           wt[:, kk, n * 512:(n + 1) * 512],
                                           start=st, stop=sp))
                            first = False
                            if m < nmm:
                                E.op(f)
                            else:
                                E.inc(s_mm, f, ("mm3", k, tb))

        def sc_SQ(E, sc, k, which):
            xap = lambda tb: sx[:, k % 3, tb, :]
            acc = s_s2a if which == 1 else s_s2b
            for tb in range(TB):
                if tb:
                    E.op(lambda: sc.drain())
                f = (lambda tb=tb: sc.activation(
                    sLNxm1[:, 0:4, :], xap(tb), AF.Square,
                    accum_out=acc[:, tb:tb + 1]))
                if tb < TB - 1:
                    E.op(f)
                else:
                    E.inc(s_sq, f, ("sq1" if which == 1 else "sq2", k))

        def sc_SQRT(E, sc, k, which):
            vin = s_vara if which == 1 else s_varb
            vout = s_sda if which == 1 else s_sdb
            E.wait(("vv1" if which == 1 else "vv2", k))
            E.inc(s_sq, lambda: sc.activation(vout[:], vin[:], AF.Sqrt),
                  ("sqrt1" if which == 1 else "sqrt2", k))

        def sc_evA(E, sc, k):
            bi = k // NBCH
            for j in range(KB):
                E.wait(("tr1", k, j))
                E.inc(s_b01, lambda j=j, bi=bi: sc.activation(
                    sLNxm1[:, j, :], pb[j % 2][:, 0:CH], AF.Identity,
                    scale=s_ca1[:, bi * 8 + j:bi * 8 + j + 1],
                    bias=s_sa[:, bi * 8 + j:bi * 8 + j + 1]),
                    ("evA", k, j))

        def sc_ev2(E, sc, k):
            bi = k // NBCH
            for j in range(KB):
                E.wait(("tr2", k, j))
                E.inc(s_b01, lambda j=j, bi=bi: sc.activation(
                    sLNxm2[:, j, :], pb[j % 2][:, 0:CH], AF.Identity,
                    scale=s_cb1[:, bi * 8 + j:bi * 8 + j + 1],
                    bias=s_sb[:, bi * 8 + j:bi * 8 + j + 1]),
                    ("ev2", k, j))

        def sc_evU(E, sc, k, i0=0, i1=KB):
            for i in range(i0, i1):
                E.wait(("mm1", k, i))
                if i >= 2:
                    E.wait(("scan", k, i - 2))
                elif k >= 1:
                    E.wait(("scan", k - 1, i + 6))
                E.inc(s_b23, lambda i=i: sc.activation(
                    su[:, i % 2, :], pb[2 + (i % 2)][:, 0:CH], AF.Identity,
                    scale=s_bv[:, i:i + 1], bias=s_bbi[:, i:i + 1]),
                    ("evU", k, i))

        def sc_TANH(E, sc, k):
            for tb in range(TB):
                E.wait(("epiFp", k, tb))
                if tb:
                    E.op(lambda: sc.drain())
                E.inc(s_th, lambda tb=tb: sc.activation(
                    sy_flat[:, tb * D:(tb + 1) * D],
                    sy_flat[:, tb * D:(tb + 1) * D], AF.Tanh),
                      ("tanh", k, tb))

        def stream_sync(E, sy):
            E.eng, E.engname = sy, "sync"
            for k in range(NCH):
                if k >= 3:
                    E.wait(("store", k - 3))
                if k >= 2:
                    E.wait(("xload", k - 2))
                E.inc([sem_x0, sem_x1][k % 2],
                      lambda k=k: sy.dma_start(out=sx[:, k % 3], in_=x_r[k]),
                      ("xload", k), by=16)

        def stream_gp(E, gp):
            E.eng, E.engname = gp, "gpsimd"
            for k in range(NCH):
                E.wait(("out", k, 1))
                if k >= 2:
                    E.wait(("store", k - 2))
                E.inc([st0, st1][k % 2],
                      lambda k=k: gp.dma_start(out=out_r[k], in_=sx[:, k % 3]),
                      ("store", k), by=16)
                if k == NBCH - 1:
                    for q, (slot, dst) in enumerate([(0, s_GA), (1, s_GAB),
                                                     (2, s_GB)]):
                        E.inc(sem_gat,
                              lambda slot=slot, dst=dst: gp.dma_start(
                                  out=dst[:], in_=gascr[slot]),
                              ("gat",) if q == 2 else None, by=16)
            E.wait(("store", NCH - 2))
            E.wait(("store", NCH - 1))

        def stream_T(E, te):
            E.eng, E.engname = te, "tensor"
            t_TR(E, te, "tr1", 0)
            for k in range(NPER):
                if k < NCH:
                    t_MM1(E, te, k)
                if 0 <= k - 1:
                    t_TR(E, te, "tr2", k - 1)
                if k + 1 < NCH:
                    t_TR(E, te, "tr1", k + 1)
                if 0 <= k - 1:
                    t_MM3(E, te, k - 1)
                if k < NCH:
                    t_MM2(E, te, k)

        def stream_V(E, ve):
            E.eng, E.engname = ve, "vector"
            v_LN1(E, ve, 0)
            E.wait(("evA", 0, 7))
            v_split(E, ve, sLNxm1, shi1, slo1, ("spA", 0))
            for k in range(NPER):
                if 0 <= k - 1:
                    v_epiA(E, ve, k - 1)
                if k < NCH:
                    v_scan(E, ve, k, 0, 2)
                if 0 <= k - 1:
                    v_LN2(E, ve, k - 1)
                if k < NCH:
                    v_scan(E, ve, k, 2, 4)
                if k + 1 < NCH:
                    v_LN1(E, ve, k + 1)
                if k < NCH:
                    v_scan(E, ve, k, 4, KB)
                if 0 <= k - 1:
                    E.wait(("ev2", k - 1, 7))
                    E.wait(("mm3", k - 2, 1) if k >= 2 else None)
                    v_split(E, ve, sLNxm2, shi2, slo2, ("sp2", k - 1))
                if k < NCH:
                    E.wait(("mm2", k - 1, 1) if k >= 1 else None)
                    v_split(E, ve, sy_t, shiY, sloY, ("spY", k))
                if k + 1 < NCH:
                    E.wait(("evA", k + 1, 7))
                    E.wait(("mm1", k, 7))
                    v_split(E, ve, sLNxm1, shi1, slo1, ("spA", k + 1))
                if 0 <= k - 1:
                    v_epiF(E, ve, k - 1)

        def stream_SC(E, sc):
            E.eng, E.engname = sc, "scalar"
            E.wait(("xload", 0))
            sc_SQ(E, sc, 0, 1)
            sc_SQRT(E, sc, 0, 1)
            sc_evA(E, sc, 0)
            for k in range(NPER):
                if k + 1 < NCH:
                    E.wait(("xload", k + 1))
                    E.wait(("vv1", k))
                    E.wait(("spA", k))
                    sc_SQ(E, sc, k + 1, 1)
                E.op(lambda: sc.drain())
                E.op(lambda: sc.activation(s_dum[:], s_A[:, 0:1], AF.Sqrt))
                if k < NCH:
                    sc_evU(E, sc, k, 0, 2)
                if 0 <= k - 1:
                    E.wait(("x2", k - 1))
                    E.wait(("vv2", k - 2) if k >= 2 else None)
                    sc_SQ(E, sc, k - 1, 2)
                if 0 <= k - 1:
                    sc_SQRT(E, sc, k - 1, 2)
                if k < NCH:
                    sc_evU(E, sc, k, 2, 4)
                if k + 1 < NCH:
                    sc_SQRT(E, sc, k + 1, 1)
                if k < NCH:
                    sc_evU(E, sc, k, 4, KB)
                if 0 <= k - 1:
                    if k < NCH:
                        E.wait(("spA", k))
                    sc_ev2(E, sc, k - 1)
                if k + 1 < NCH:
                    sc_evA(E, sc, k + 1)
                if 0 <= k - 1:
                    sc_TANH(E, sc, k - 1)

        E.plan = True
        stream_sync(E, None)
        stream_gp(E, None)
        stream_T(E, None)
        stream_V(E, None)
        stream_SC(E, None)

        E.plan = False
        EMIT_LOG.append(("__BLOCK__", "bar", "", 0))
        with nc.Block() as blk:
            @blk.sync
            def _(sy):
                stream_sync(E, sy)

            @blk.gpsimd
            def _(gp):
                stream_gp(E, gp)

            @blk.tensor
            def _(te):
                stream_T(E, te)

            @blk.vector
            def _(ve):
                stream_V(E, ve)

            @blk.scalar
            def _(sc):
                stream_SC(E, sc)

        if debug:
            dbg2 = [
                ("sx", sx[:, 0:2], F32), ("sLNx", sLNxA, F32), ("s_s1a", s_s1a, F32),
                ("s_s2a", s_s2a, F32), ("s_vara", s_vara, F32),
                ("s_sda", s_sda, F32), ("s_ra", s_ra, F32), ("s_nma", s_nma, F32),
                ("sLNxm", sLNxm1, F32), ("shiA", shi1, BF16), ("sloA", slo1, BF16),
                ("su", su, F32), ("sy_t", sy_t, F32), ("shiY", shiY, BF16),
                
                ("s_s1b", s_s1b, F32), ("s_varb", s_varb, F32),
                ("s_rb", s_rb, F32), ("scarry", scarry, F32),
            ]
            d2_outs = {}
            from concourse.bass import AP as _APc
            def _ap(t):
                return t if isinstance(t, _APc) else t[:]
            for nm, t, dt_ in dbg2:
                ap = _ap(t)
                d2_outs[nm] = nc.dram_tensor("dbg2_" + nm, list(ap.shape), dt_,
                                             kind="ExternalOutput").ap()
            EMIT_LOG.append(("__BLOCK__", "bar", "", 0))
            with nc.Block() as blk:
                @blk.gpsimd
                def _(gp):
                    n = 0
                    for nm, t, dt_ in dbg2:
                        I(gp.dma_start(out=d2_outs[nm], in_=_ap(t)), "gpsimd", u4, 16)
                        n += 1
                    W(gp, "gpsimd", u4, G["u4"] + 16 * n)
            G["u4"] += 16 * len(dbg2)

    return nc


def kernel(**inputs):
    x = np.ascontiguousarray(np.asarray(inputs["L0_attn_x"], dtype=np.float32))
    w_in = np.asarray(inputs["L0_attn_w_in"], dtype=np.float32)
    w_out = np.asarray(inputs["L0_attn_w_out"], dtype=np.float32)
    wff = np.asarray(inputs["L0_ffn_weight"], dtype=np.float32)
    a1 = np.ascontiguousarray(np.asarray(inputs["L0_ffn_aux_weight1"], dtype=np.float32))
    a2 = np.ascontiguousarray(np.asarray(inputs["L0_ffn_aux_weight2"], dtype=np.float32))
    wmod = np.ascontiguousarray(np.asarray(inputs["L0_wmod"], dtype=np.float32))
    cmat = np.asarray(inputs["L0_c"], dtype=np.float32)

    winT = np.ascontiguousarray(w_in.T)
    woutT = np.ascontiguousarray(w_out.T)
    wT = np.ascontiguousarray(wff.T)
    ident = np.eye(128, dtype=np.float32)
    ones1 = np.ones((1, 128), dtype=np.float32)
    avec = np.ascontiguousarray(np.asarray(inputs["L0_attn_a"], dtype=np.float32))
    bvec = np.ascontiguousarray(np.asarray(inputs["L0_attn_b"], dtype=np.float32))
    binv = np.ascontiguousarray(np.asarray(inputs["L0_attn_b_in"], dtype=np.float32))
    boutv = np.ascontiguousarray(np.asarray(inputs["L0_attn_b_out"], dtype=np.float32))
    fbias = np.ascontiguousarray(np.asarray(inputs["L0_ffn_bias"], dtype=np.float32))

    nc = build_program()
    in_maps = []
    for core in range(8):
        xc = np.ascontiguousarray(x[2 * core:2 * core + 2].reshape(TC, D))
        cT = np.ascontiguousarray(cmat[2 * core:2 * core + 2].T)
        in_maps.append(dict(
            x=xc, winT=winT, woutT=woutT, wT=wT, a1=a1, a2=a2, wmod=wmod,
            cT=cT, avec=avec, bvec=bvec, binv=binv, boutv=boutv, fbias=fbias,
            ident=ident, ones1=ones1,
        ))
    res = run_bass_kernel_spmd(nc, in_maps, core_ids=list(range(8)))
    out = np.concatenate([r["out"] for r in res.results], axis=0)
    return out.reshape(16, 4096, D).astype(np.float32)


# version 20
# speedup vs baseline: 1.0316x; 1.0282x over previous
import contextlib
import numpy as np

import concourse.bass as bass
import concourse.mybir as mybir
from concourse.bass_utils import run_bass_kernel_spmd

F32 = mybir.dt.float32
BF16 = mybir.dt.bfloat16
AF = mybir.ActivationFunctionType
ALU = mybir.AluOpType
AX = mybir.AxisListType

D = 1024
KB = 8
TC = 8192
CH = 256
TB = CH // 128
NCH = TC // CH
NBCH = NCH // 2

EMIT_LOG = []


def build_program(tc=TC, debug=False):
    EMIT_LOG.clear()
    NCH = tc // CH
    NBCH = max(1, NCH // 2)
    nc = bass.Bass("TRN2", target_bir_lowering=False)

    x_h = nc.dram_tensor("x", [tc, D], F32, kind="ExternalInput").ap()
    winT_h = nc.dram_tensor("winT", [D, D], F32, kind="ExternalInput").ap()
    woutT_h = nc.dram_tensor("woutT", [D, D], F32, kind="ExternalInput").ap()
    wT_h = nc.dram_tensor("wT", [D, D], F32, kind="ExternalInput").ap()
    a1_h = nc.dram_tensor("a1", [D, D], F32, kind="ExternalInput").ap()
    a2_h = nc.dram_tensor("a2", [D, D], F32, kind="ExternalInput").ap()
    wmod_h = nc.dram_tensor("wmod", [D, 6 * D], F32, kind="ExternalInput").ap()
    cT_h = nc.dram_tensor("cT", [D, 2], F32, kind="ExternalInput").ap()
    avec_h = nc.dram_tensor("avec", [D], F32, kind="ExternalInput").ap()
    bvec_h = nc.dram_tensor("bvec", [D], F32, kind="ExternalInput").ap()
    bin_h = nc.dram_tensor("binv", [D], F32, kind="ExternalInput").ap()
    bout_h = nc.dram_tensor("boutv", [D], F32, kind="ExternalInput").ap()
    fbias_h = nc.dram_tensor("fbias", [D], F32, kind="ExternalInput").ap()
    ident_h = nc.dram_tensor("ident", [128, 128], F32, kind="ExternalInput").ap()
    ones_h = nc.dram_tensor("ones1", [1, 128], F32, kind="ExternalInput").ap()
    out_h = nc.dram_tensor("out", [tc, D], F32, kind="ExternalOutput").ap()
    t1t_d = nc.dram_tensor("t1t_d", [D, D], F32).ap()
    modscr = nc.dram_tensor("modscr", [2, 6 * D], F32).ap()
    gabscr = nc.dram_tensor("gabscr", [2, D], F32).ap()
    gascr = nc.dram_tensor("gascr", [3, 128, D], F32).ap()

    with contextlib.ExitStack() as stp:
        def sb(name, shape, dt=F32):
            return stp.enter_context(nc.sbuf_tensor(name, shape, dt))

        s_winT_hi = sb("s_winT_hi", [128, KB, D], BF16)
        s_winT_lo = sb("s_winT_lo", [128, KB, D], BF16)
        s_woutT_hi = sb("s_woutT_hi", [128, KB, D], BF16)
        s_woutT_lo = sb("s_woutT_lo", [128, KB, D], BF16)
        s_Wc_hi = sb("s_Wc_hi", [128, KB, D], BF16)
        s_Wc_lo = sb("s_Wc_lo", [128, KB, D], BF16)
        s_id = sb("s_id", [128, 128])
        s_ones = sb("s_ones", [1, 128])
        s_GA = sb("s_GA", [128, D])
        s_GAB = sb("s_GAB", [128, D])
        s_GB = sb("s_GB", [128, D])
        s_BIASC = sb("s_BIASC", [128, D])
        s_ca1 = sb("s_ca1", [128, 16])
        s_sa = sb("s_sa", [128, 16])
        s_cb1 = sb("s_cb1", [128, 16])
        s_sb = sb("s_sb", [128, 16])
        s_A = sb("s_A", [128, KB])
        s_bv = sb("s_bv", [128, KB])
        s_binv = sb("s_binv", [128, KB])
        s_bbi = sb("s_bbi", [128, KB])
        scarry = sb("scarry", [128, KB])
        s_cT = sb("s_cT", [128, KB, 2])
        s_silc = sb("s_silc", [128, KB, 2])
        s_av = sb("s_av", [128, KB])
        s_s1a = sb("s_s1a", [128, TB]); s_s2a = sb("s_s2a", [128, TB])
        s_s1b = sb("s_s1b", [128, TB]); s_s2b = sb("s_s2b", [128, TB])
        s_vara = sb("s_vara", [128, TB]); s_varb = sb("s_varb", [128, TB])
        s_sda = sb("s_sda", [128, TB]); s_sdb = sb("s_sdb", [128, TB])
        s_ra = sb("s_ra", [128, TB]); s_rb = sb("s_rb", [128, TB])
        s_nma = sb("s_nma", [128, TB]); s_nmb = sb("s_nmb", [128, TB])
        s_q2 = sb("s_q2", [128, TB])

        pb = [stp.enter_context(nc.psum_tensor(f"pb{i}", [128, 512], F32))
              for i in range(8)]

        SEMS = {}
        NAME = {}
        def sem(name):
            s = stp.enter_context(nc.semaphore(name))
            SEMS[name] = s
            NAME[id(s)] = name
            return s

        sem_x = sem("sem_x"); sem_st = sem("sem_st"); sem_gat = sem("sem_gat")
        s_v = sem("s_v"); s_vv = sem("s_vv"); s_sq = sem("s_sq")
        s_tr = sem("s_tr"); s_b01 = sem("s_b01"); s_sp = sem("s_sp")
        s_mm = sem("s_mm"); s_b23 = sem("s_b23"); s_b47 = sem("s_b47")
        s_th = sem("s_th"); s_out = sem("s_out")
        s_scan = sem("s_scan")
        u1 = sem("u1"); u2 = sem("u2"); u3 = sem("u3"); u4 = sem("u4")
        d0 = sem("d0"); d1 = sem("d1")
        sem_x0 = sem("sem_x0"); sem_x1 = sem("sem_x1")
        st0 = sem("st0"); st1 = sem("st1")

        def W(eng, engname, smp, val):
            EMIT_LOG.append((engname, "w", NAME[id(smp)], val))
            if val > 0:
                eng.wait_ge(smp, val)

        def I(ins, engname, smp, val=1):
            EMIT_LOG.append((engname, "i", NAME[id(smp)], val))
            return ins.then_inc(smp, val)

        G = {"u1": 0, "u2": 0, "u3": 0, "u4": 0}

        with contextlib.ExitStack() as st2:
            def sb2(name, shape, dt=F32):
                return st2.enter_context(nc.sbuf_tensor(name, shape, dt))
            B1 = sb2("B1", [128, KB, D])
            B2h = sb2("B2h", [128, KB, D], BF16)
            B2l = sb2("B2l", [128, KB, D], BF16)
            B3h = s_woutT_hi
            B3l = s_woutT_lo
            WCSh = s_winT_hi
            WCSl = s_winT_lo
            s_rowm = sb2("s_rowm", [2, 1024])
            u5 = sem("u5")
            u6 = sem("u6")

            wmod_r = wmod_h.rearrange("(k p) m -> p k m", p=128)
            a1_r = a1_h.rearrange("(k p) c -> p k c", p=128)
            wt_r = wT_h.rearrange("(k p) c -> p k c", p=128)
            a2_r = a2_h.rearrange("(k p) c -> p k c", p=128)
            T1T_UNITS = ([(j, 0) for j in range(8)]
                         + [(j, 1) for j in range(8)])
            AVAIL = [2] * 4 + [5] * 4 + [7] * 8
            V_ORDER = (["suba1h0", "subwth0", "ev0", "ev1", "suba1h1", "ev2",
                        "subwth1"] + [f"ev{p}" for p in range(3, 16)] + ["a2sub"]
                       + [f"wce{m}" for m in range(16)] + ["cph", "cpl"])
            V2 = {nm: i + 1 for i, nm in enumerate(V_ORDER)}
            EMIT_LOG.append(("__BLOCK__", "bar", "", 0))
            with nc.Block() as blk:
                @blk.sync
                def _(sy):
                    I(sy.dma_start(out=B1[:, :, 0:512], in_=a1_r[:, :, 0:512]),
                      "sync", u6, 16)
                    small = [
                        (s_id[:], ident_h),
                        (s_ones[:], ones_h),
                        (s_cT[:], cT_h.rearrange("(k p) b -> p k b", p=128)),
                        (s_av[:], avec_h.rearrange("(k p) -> p k", p=128)),
                        (s_bv[:], bvec_h.rearrange("(k p) -> p k", p=128)),
                        (s_binv[:], bin_h.rearrange("(k p) -> p k", p=128)),
                    ]
                    with nc.allow_non_contiguous_dma(reason="small setup vectors"):
                        for dst, srcap in small:
                            I(sy.dma_start(out=dst, in_=srcap), "sync", u1, 16)
                    W(sy, "sync", u2, V2["suba1h0"])
                    I(sy.dma_start(out=B1[:, :, 0:512], in_=a1_r[:, :, 512:1024]),
                      "sync", u6, 16)
                    W(sy, "sync", u2, V2["subwth1"])
                    I(sy.dma_start(out=B1[:], in_=a2_r), "sync", u6, 16)
                    for k in range(8):
                        L = 2 * k
                        W(sy, "sync",
                          u2 if L == 0 else u3,
                          V2["a2sub"] if L == 0 else 31 + L)
                        I(sy.dma_start(
                            out=B1[:, 0:3, :],
                            in_=wmod_h[k * 128:(k + 1) * 128, 0:3072].rearrange(
                                "p (a c) -> p a c", a=3)), "sync", d0, 16)

                @blk.scalar
                def _(sc):
                    W(sc, "scalar", u6, 16)
                    I(sc.activation(B2h[:, :, 0:512], B1[:, :, 0:512],
                                    AF.Identity), "scalar", u5)
                    W(sc, "scalar", u4, 16)
                    I(sc.activation(B3h[:, :, 0:512], B1[:, :, 512:1024],
                                    AF.Identity), "scalar", u5)
                    W(sc, "scalar", u1, 16 * 6)
                    sc.activation(s_silc[:], s_cT[:], AF.Silu)
                    I(sc.activation(s_A[:], s_av[:], AF.Sigmoid), "scalar", u5)
                    W(sc, "scalar", u6, 32)
                    I(sc.activation(B2h[:, :, 512:1024], B1[:, :, 0:512],
                                    AF.Identity), "scalar", u5)
                    W(sc, "scalar", u4, 32)
                    I(sc.activation(B3h[:, :, 512:1024], B1[:, :, 512:1024],
                                    AF.Identity), "scalar", u5)
                    W(sc, "scalar", u6, 48)
                    W(sc, "scalar", u3, 16)
                    I(sc.activation(B2h[:], B1[:], AF.Identity), "scalar", u5)
                    W(sc, "scalar", u3, 48)
                    for n in range(12):
                        if n >= 2:
                            W(sc, "scalar", u4, 48 + 16 * (n - 2))
                        I(sc.activation(
                            s_rowm[:, (n % 2) * 512:(n % 2) * 512 + 512],
                            pb[n % 6][(n // 6) * 32:(n // 6) * 32 + 2, :],
                            AF.Identity), "scalar", u5)

                @blk.vector
                def _(ve):
                    for nm in V_ORDER:
                        if nm == "suba1h0":
                            W(ve, "vector", u5, 1)
                            I(ve.tensor_tensor(B2l[:, :, 0:512], B1[:, :, 0:512],
                                               B2h[:, :, 0:512], ALU.subtract),
                              "vector", u2)
                        elif nm == "subwth0":
                            W(ve, "vector", u5, 2)
                            I(ve.tensor_tensor(B3l[:, :, 0:512],
                                               B1[:, :, 512:1024],
                                               B3h[:, :, 0:512], ALU.subtract),
                              "vector", u2)
                            W(ve, "vector", u1, 16 * 6)
                            ve.tensor_tensor(s_bbi[:], s_bv[:], s_binv[:],
                                             ALU.mult)
                        elif nm == "suba1h1":
                            W(ve, "vector", u5, 4)
                            I(ve.tensor_tensor(B2l[:, :, 512:1024],
                                               B1[:, :, 0:512],
                                               B2h[:, :, 512:1024], ALU.subtract),
                              "vector", u2)
                        elif nm == "subwth1":
                            W(ve, "vector", u5, 5)
                            I(ve.tensor_tensor(B3l[:, :, 512:1024],
                                               B1[:, :, 512:1024],
                                               B3h[:, :, 512:1024], ALU.subtract),
                              "vector", u2)
                        elif nm == "a2sub":
                            W(ve, "vector", u5, 6)
                            I(ve.tensor_tensor(B2l[:], B1[:], B2h[:],
                                               ALU.subtract), "vector", u2)
                        elif nm == "cph":
                            W(ve, "vector", u3, 32)
                            ve.drain()
                            I(ve.tensor_copy(s_Wc_hi[:], WCSh[:]), "vector", u2)
                        elif nm == "cpl":
                            I(ve.tensor_copy(s_Wc_lo[:], WCSl[:]), "vector", u2)
                        elif nm.startswith("ev"):
                            p = int(nm[2:])
                            j, n = T1T_UNITS[p]
                            W(ve, "vector", u3, p + 1)
                            hi = s_Wc_hi[:, j, n * 512:(n + 1) * 512]
                            lo = s_Wc_lo[:, j, n * 512:(n + 1) * 512]
                            ve.tensor_copy(hi, pb[p % 2][:])
                            ve.drain()
                            I(ve.tensor_tensor(lo, pb[p % 2][:], hi,
                                               ALU.subtract), "vector", u2)
                        else:
                            m = int(nm[3:])
                            j, n = m // 2, m % 2
                            W(ve, "vector", u3, 16 + m + 1)
                            hi = WCSh[:, j, n * 512:(n + 1) * 512]
                            lo = WCSl[:, j, n * 512:(n + 1) * 512]
                            ve.tensor_copy(hi, pb[m % 2][:])
                            ve.drain()
                            I(ve.tensor_tensor(lo, pb[m % 2][:], hi,
                                               ALU.subtract), "vector", u2)

                @blk.tensor
                def _(te):
                    for p, (j, n) in enumerate(T1T_UNITS):
                        w = AVAIL[p]
                        if p >= 2:
                            w = max(w, V2[f"ev{p-2}"])
                        W(te, "tensor", u2, w)
                        last = None
                        first = True
                        for lh, rh in [(B2h, B3h), (B2h, B3l), (B2l, B3h)]:
                            for k in range(KB):
                                last = te.matmul(
                                    pb[p % 2][:], lh[:, k, j * 128:(j + 1) * 128],
                                    rh[:, k, n * 512:(n + 1) * 512],
                                    start=first, stop=(lh is B2l and k == KB - 1))
                                first = False
                        I(last, "tensor", u3)
                    for m in range(16):
                        j, n = m // 2, m % 2
                        w = V2["a2sub"]
                        if m >= 2:
                            w = max(w, V2[f"wce{m-2}"])
                        W(te, "tensor", u2, w)
                        last = None
                        first = True
                        for lh, rh in [(s_Wc_hi, B2h), (s_Wc_hi, B2l),
                                       (s_Wc_lo, B2h)]:
                            for k in range(KB):
                                last = te.matmul(
                                    pb[m % 2][:], lh[:, k, j * 128:(j + 1) * 128],
                                    rh[:, k, n * 512:(n + 1) * 512],
                                    start=first, stop=(lh is s_Wc_lo and k == KB - 1))
                                first = False
                        I(last, "tensor", u3)
                    W(te, "tensor", u5, 3)
                    W(te, "tensor", u2, V2["wce15"])
                    for L in range(16):
                        k, h = L // 2, L % 2
                        W(te, "tensor", [d0, d1][h], 16 * (L // 2 + 1))
                        last = None
                        for j in range(6):
                            c = 6 * h + j
                            last = te.matmul(
                                pb[c % 6][(c // 6) * 32:(c // 6) * 32 + 2, :],
                                s_silc[:, k, :],
                                B1[:, 3 * h + j // 2,
                                   (j % 2) * 512:(j % 2) * 512 + 512],
                                start=(L < 2), stop=(k == KB - 1))
                        I(last, "tensor", u3)

                @blk.gpsimd
                def _(gp):
                    I(gp.dma_start(out=B1[:, :, 512:1024], in_=wt_r[:, :, 0:512]),
                      "gpsimd", u4, 16)
                    W(gp, "gpsimd", u2, V2["subwth0"])
                    I(gp.dma_start(out=B1[:, :, 512:1024],
                                   in_=wt_r[:, :, 512:1024]), "gpsimd", u4, 16)
                    for k in range(8):
                        L = 2 * k + 1
                        W(gp, "gpsimd",
                          u2 if L == 1 else u3,
                          V2["a2sub"] if L == 1 else 31 + L)
                        I(gp.dma_start(
                            out=B1[:, 3:6, :],
                            in_=wmod_h[k * 128:(k + 1) * 128, 3072:6144]
                            .rearrange("p (a c) -> p a c", a=3)), "gpsimd",
                          d1, 16)
                    for n in range(12):
                        W(gp, "gpsimd", u5, 6 + n + 1)
                        W(gp, "gpsimd", u4, 32 + 16 * n)
                        I(gp.dma_start(
                            out=modscr[:, n * 512:(n + 1) * 512],
                            in_=s_rowm[:, (n % 2) * 512:(n % 2) * 512 + 512]),
                          "gpsimd", u4, 16)
                    W(gp, "gpsimd", u4, 16 * 14)
            G["u1"] = 16 * 6; G["u2"] = len(V_ORDER); G["u3"] = 48
            G["u4"] = 16 * 14

        with contextlib.ExitStack() as st2:
            def sb2(name, shape, dt=F32):
                return st2.enter_context(nc.sbuf_tensor(name, shape, dt))
            B4 = sb2("B4", [128, KB, D])
            s_rga = sb2("s_rga", [2, D])
            s_rgb = sb2("s_rgb", [2, D])
            s_bout2 = sb2("s_bout2", [2, D])
            s_row2 = sb2("s_row2", [2, D])
            s_mask = sb2("s_mask", [2, D])
            s_sgn = sb2("s_sgn", [2, D])
            s_row1 = s_row2[0:1, :]
            s_rgab = s_sgn

            B4b = sb2("B4b", [128, KB, D])
            s_rowB = sb2("s_rowB", [1, D])
            bc_jobs = [
                (s_GA, 0, modscr[1, 2 * D:3 * D]),
                (s_GAB, 1, gabscr[1, :]),
                (s_GB, 2, modscr[1, 5 * D:6 * D]),
                (s_GA, None, modscr[0, 2 * D:3 * D]),
                (s_GAB, None, gabscr[0, :]),
                (s_GB, None, modscr[0, 5 * D:6 * D]),
                (s_BIASC, None, fbias_h),
            ]
            b1 = G["u1"]; b2 = G["u2"]; b3 = G["u3"]; b4 = G["u4"]; b5 = 18
            u7 = sem("u7")
            winT_r = winT_h.rearrange("(k p) e -> p k e", p=128)
            woutT_r = woutT_h.rearrange("(k p) e -> p k e", p=128)
            EMIT_LOG.append(("__BLOCK__", "bar", "", 0))
            with nc.Block() as blk:
                @blk.sync
                def _(sy):
                    one = lambda ap: ap.rearrange("(a d) -> a d", a=1)
                    I(sy.dma_start(out=s_rga[:], in_=modscr[:, 2 * D:3 * D]),
                      "sync", u1, 16)
                    I(sy.dma_start(out=s_rgb[:], in_=modscr[:, 5 * D:6 * D]),
                      "sync", u1, 16)
                    I(sy.dma_start(out=s_bout2[0:1, :], in_=one(bout_h)),
                      "sync", u1, 16)
                    I(sy.dma_start(out=s_bout2[1:2, :], in_=one(bout_h)),
                      "sync", u1, 16)
                    W(sy, "sync", u4, b4 + 64)
                    for q in range(7):
                        W(sy, "sync", u1, b1 + 64 + 16 * q)
                        if q >= 2:
                            W(sy, "sync", u3, b3 + 2 * (q - 1))
                        I(sy.dma_start(out=[s_row1, s_rowB[:]][q % 2],
                                       in_=one(bc_jobs[q][2])), "sync", u1, 16)
                    W(sy, "sync", u1, b1 + 176)
                    I(sy.dma_start(out=B4[:, :, 0:512],
                                   in_=winT_r[:, :, 0:512]), "sync", u1, 16)
                    W(sy, "sync", u1, b1 + 192)
                    I(sy.dma_start(out=B4[:, :, 512:1024],
                                   in_=winT_r[:, :, 512:1024]), "sync", u1, 16)
                    with nc.allow_non_contiguous_dma(reason="small mod vectors"):
                        for bi in range(2):
                            for dst, off in [(s_sa, 0), (s_ca1, 1),
                                             (s_sb, 3), (s_cb1, 4)]:
                                srcv = modscr[bi, off * D:(off + 1) * D]
                                I(sy.dma_start(
                                    out=dst[:, bi * 8:(bi + 1) * 8],
                                    in_=srcv.rearrange("(k p) -> p k", p=128)),
                                  "sync", u7, 16)

                @blk.gpsimd
                def _(gp):
                    I(gp.dma_start(out=B4b[:, :, 0:512],
                                   in_=woutT_r[:, :, 0:512]), "gpsimd", u4, 16)
                    W(gp, "gpsimd", u2, b2 + 1)
                    W(gp, "gpsimd", u4, b4 + 16)
                    I(gp.dma_start(out=modscr[:, 2 * D:3 * D], in_=s_rga[:]),
                      "gpsimd", u4, 16)
                    W(gp, "gpsimd", u4, b4 + 32)
                    I(gp.dma_start(out=modscr[:, 5 * D:6 * D], in_=s_rgb[:]),
                      "gpsimd", u4, 16)
                    W(gp, "gpsimd", u4, b4 + 48)
                    I(gp.dma_start(out=gabscr[:], in_=s_rgab[:]), "gpsimd",
                      u4, 16)
                    for q2, (slot, dst) in enumerate([(0, s_GA), (1, s_GAB),
                                                      (2, s_GB)]):
                        W(gp, "gpsimd", u5, b5 + 2 * q2 + 2)
                        W(gp, "gpsimd", u4, b4 + 64 + 16 * q2)
                        I(gp.dma_start(out=gascr[slot], in_=dst[:]), "gpsimd",
                          u4, 16)
                    W(gp, "gpsimd", u4, b4 + 112)
                    I(gp.dma_start(out=B4b[:, :, 512:1024],
                                   in_=woutT_r[:, :, 512:1024]), "gpsimd", u4, 16)
                    W(gp, "gpsimd", u4, b4 + 128)

                @blk.tensor
                def _(te):
                    for q in range(7):
                        W(te, "tensor", u1, b1 + 64 + 16 * (q + 1))
                        for n in range(2):
                            m = 2 * q + n
                            if m >= 2:
                                W(te, "tensor", u5, b5 + m - 1)
                            rb = [s_row1, s_rowB[:]][q % 2]
                            I(te.matmul(
                                pb[m % 2][:], s_ones[:],
                                rb[:, n * 512:(n + 1) * 512],
                                start=True, stop=True), "tensor", u3)

                @blk.scalar
                def _(sc):
                    for m in range(14):
                        q, n = m // 2, m % 2
                        W(sc, "scalar", u3, b3 + m + 1)
                        if m in (6, 8, 10):
                            W(sc, "scalar", u4, b4 + 80 + 16 * ((m - 6) // 2))
                        dst = bc_jobs[q][0]
                        I(sc.activation(dst[:, n * 512:(n + 1) * 512],
                                        pb[m % 2][:], AF.Identity), "scalar", u5)
                    W(sc, "scalar", u4, b4 + 16)
                    I(sc.activation(s_woutT_hi[:, :, 0:512], B4b[:, :, 0:512],
                                    AF.Identity), "scalar", u5)
                    W(sc, "scalar", u1, b1 + 192)
                    I(sc.activation(s_winT_hi[:, :, 0:512], B4[:, :, 0:512],
                                    AF.Identity), "scalar", u5)
                    W(sc, "scalar", u4, b4 + 128)
                    I(sc.activation(s_woutT_hi[:, :, 512:1024],
                                    B4b[:, :, 512:1024], AF.Identity),
                      "scalar", u5)
                    W(sc, "scalar", u1, b1 + 208)
                    I(sc.activation(s_winT_hi[:, :, 512:1024],
                                    B4[:, :, 512:1024], AF.Identity),
                      "scalar", u5)

                @blk.vector
                def _(ve):
                    W(ve, "vector", u1, b1 + 64)
                    for rr in (s_rga, s_rgb):
                        ve.tensor_single_scalar(s_mask[:], rr[:], 0.0, ALU.is_ge)
                        ve.drain()
                        ve.tensor_scalar(s_sgn[:], s_mask[:], 2.0, -1.0,
                                         ALU.mult, ALU.add)
                        ve.drain()
                        ve.tensor_tensor(s_mask[:], rr[:], s_sgn[:], ALU.mult)
                        ve.drain()
                        ve.tensor_scalar_max(s_mask[:], s_mask[:], 0.25)
                        ve.drain()
                        ve.tensor_tensor(rr[:], s_mask[:], s_sgn[:], ALU.mult)
                        ve.drain()
                    I(ve.tensor_tensor(s_rgab[:], s_rga[:], s_bout2[:],
                                       ALU.mult), "vector", u2)
                    for h, (lo, src, hi, wv) in enumerate([
                            (s_woutT_lo, B4b, s_woutT_hi, b5 + 15),
                            (s_winT_lo, B4, s_winT_hi, b5 + 16),
                            (s_woutT_lo, B4b, s_woutT_hi, b5 + 17),
                            (s_winT_lo, B4, s_winT_hi, b5 + 18)]):
                        c0 = (h // 2) * 512
                        W(ve, "vector", u5, wv)
                        I(ve.tensor_tensor(lo[:, :, c0:c0 + 512],
                                           src[:, :, c0:c0 + 512],
                                           hi[:, :, c0:c0 + 512], ALU.subtract),
                          "vector", u2)
                    W(ve, "vector", u7, 16 * 8)
                    ve.tensor_scalar_add(s_ca1[:], s_ca1[:], 1.0)
                    ve.drain()
                    ve.tensor_scalar_add(s_cb1[:], s_cb1[:], 1.0)
            G["u1"] += 208; G["u2"] += 5; G["u3"] += 14; G["u4"] += 128

        if debug:
            dbg_specs = [
                ("s_A", s_A, F32), ("s_bbi", s_bbi, F32), ("s_silc", s_silc, F32),
                ("s_ca1", s_ca1, F32), ("s_sa", s_sa, F32),
                ("s_cb1", s_cb1, F32), ("s_sb", s_sb, F32),
                ("s_GA", s_GA, F32), ("s_GAB", s_GAB, F32), ("s_GB", s_GB, F32),
                ("s_BIASC", s_BIASC, F32),
                ("s_winT_hi", s_winT_hi, BF16), ("s_winT_lo", s_winT_lo, BF16),
                ("s_woutT_hi", s_woutT_hi, BF16),
                ("s_Wc_hi", s_Wc_hi, BF16), ("s_Wc_lo", s_Wc_lo, BF16),
                ("s_id", s_id, F32), ("s_ones", s_ones, F32),
            ]
            dbg_outs = {}
            for nm, t, dt_ in dbg_specs:
                shp = list(t.shape) if hasattr(t, "shape") else None
                ap = t[:]
                dbg_outs[nm] = nc.dram_tensor("dbg_" + nm, list(ap.shape), dt_,
                                              kind="ExternalOutput").ap()
            EMIT_LOG.append(("__BLOCK__", "bar", "", 0))
            with nc.Block() as blk:
                @blk.gpsimd
                def _(gp):
                    n = 0
                    for nm, t, dt_ in dbg_specs:
                        I(gp.dma_start(out=dbg_outs[nm], in_=t[:]), "gpsimd", u4, 16)
                        n += 1
                    W(gp, "gpsimd", u4, G["u4"] + 16 * n)
            G["u4"] += 16 * len(dbg_specs)

        sx = sb("sx", [128, 3, TB, D])
        sLNxA = sb("sLNxA", [128, TB, D])
        sLNxB = sb("sLNxB", [128, TB, D])
        sLNxm1 = sb("sLNxm1", [128, KB, CH])
        sLNxm2 = sb("sLNxm2", [128, KB, CH])
        shi1 = sb("shi1", [128, KB, CH], BF16)
        slo1 = sb("slo1", [128, KB, CH], BF16)
        shi2 = sb("shi2", [128, KB, CH], BF16)
        slo2 = sb("slo2", [128, KB, CH], BF16)
        su = sb("su", [128, 2, CH])
        stmp = sb("stmp", [128, 512])
        s_dum = sb("s_dum", [128, 1])
        sy_t = sb("sy_t", [128, KB, CH])
        shiY = sb("shiY", [128, KB, CH], BF16)
        sloY = sb("sloY", [128, KB, CH], BF16)

        x_r = x_h.rearrange("(c tb p) d -> c p tb d", p=128, tb=TB)
        out_r = out_h.rearrange("(c tb p) d -> c p tb d", p=128, tb=TB)

        sy_flat = sy_t[:].rearrange("p a b -> p (a b)")

        class EV:
            def __init__(self):
                self.counts = {}
                self.events = {}
                self.plan = True
                self.eng = None
                self.engname = ""

            def op(self, f):
                if not self.plan:
                    f()

            def inc(self, sem, f, key=None, by=1):
                nm = NAME[id(sem)]
                if self.plan:
                    v = self.counts.get(nm, 0) + by
                    self.counts[nm] = v
                    if key is not None:
                        assert key not in self.events, key
                        self.events[key] = (sem, v)
                else:
                    EMIT_LOG.append((self.engname, "i", nm, by))
                    f().then_inc(sem, by)

            def wait(self, key):
                if key is None or self.plan:
                    return
                sem, v = self.events[key]
                EMIT_LOG.append((self.engname, "w", NAME[id(sem)], v))
                self.eng.wait_ge(sem, v)

        E = EV()
        NPER = NCH + 1

        seq01 = [("tr1", 0)]
        for k in range(NPER):
            if 0 <= k - 1 < NCH:
                seq01.append(("tr2", k - 1))
            if k + 1 < NCH:
                seq01.append(("tr1", k + 1))
        EVN = {"tr1": "evA", "tr2": "ev2"}
        prev01 = {}
        lastb = {0: None, 1: None}
        for kind, kk in seq01:
            for j in range(KB):
                prev01[(kind, kk, j)] = lastb[j % 2]
                lastb[j % 2] = (EVN[kind], kk, j)

        def prev23(k, i):
            if i >= 2:
                return ("evU", k, i - 2)
            if k >= 1:
                return ("evU", k - 1, 6 + i)
            return None

        def v_LN1(E, ve, k):
            E.wait(("xload", k))
            for tb in range(TB):
                E.op(lambda tb=tb: ve.reduce_sum(
                    s_s1a[:, tb:tb + 1], sx[:, k % 3, tb, :], axis=AX.X))
            E.op(lambda: ve.drain())
            E.wait(("sq1", k))
            E.op(lambda: ve.tensor_scalar_mul(s_nma[:], s_s1a[:], 1.0 / D))
            E.op(lambda: ve.tensor_scalar(s_vara[:], s_s2a[:], 1.0 / D, 1e-5,
                                          ALU.mult, ALU.add))
            E.op(lambda: ve.drain())
            E.op(lambda: ve.tensor_tensor(s_q2[:], s_nma[:], s_nma[:], ALU.mult))
            E.op(lambda: ve.drain())
            E.inc(s_vv, lambda: ve.tensor_tensor(s_vara[:], s_vara[:], s_q2[:],
                                                 ALU.subtract), ("vv1", k))
            E.wait(("sqrt1", k))
            E.op(lambda: ve.reciprocal(s_ra[:], s_sda[:]))
            E.op(lambda: ve.drain())
            E.op(lambda: ve.scalar_tensor_tensor(s_nma[:], s_nma[:], -1.0, s_ra[:],
                                                 ALU.mult, ALU.mult))
            E.op(lambda: ve.drain())
            if k >= 1:
                E.wait(("tr1", k - 1, 7))
            for tb in range(TB):
                f = (lambda tb=tb: ve.tensor_scalar(
                    sLNxB[:, tb, :], sx[:, k % 3, tb, :],
                    s_ra[:, tb:tb + 1], s_nma[:, tb:tb + 1], ALU.mult, ALU.add))
                if tb < TB - 1:
                    E.op(f)
                else:
                    E.inc(s_v, f, ("c1", k))

        def v_LN2(E, ve, k):
            for tb in range(TB):
                E.op(lambda tb=tb: ve.reduce_sum(
                    s_s1b[:, tb:tb + 1], sx[:, k % 3, tb, :], axis=AX.X))
            E.op(lambda: ve.drain())
            E.wait(("sq2", k))
            E.op(lambda: ve.tensor_scalar_mul(s_nmb[:], s_s1b[:], 1.0 / D))
            E.op(lambda: ve.tensor_scalar(s_varb[:], s_s2b[:], 1.0 / D, 1e-5,
                                          ALU.mult, ALU.add))
            E.op(lambda: ve.drain())
            E.op(lambda: ve.tensor_tensor(s_q2[:], s_nmb[:], s_nmb[:], ALU.mult))
            E.op(lambda: ve.drain())
            E.inc(s_vv, lambda: ve.tensor_tensor(s_varb[:], s_varb[:], s_q2[:],
                                                 ALU.subtract), ("vv2", k))
            E.wait(("sqrt2", k))
            E.op(lambda: ve.reciprocal(s_rb[:], s_sdb[:]))
            E.op(lambda: ve.drain())
            E.op(lambda: ve.scalar_tensor_tensor(s_nmb[:], s_nmb[:], -1.0, s_rb[:],
                                                 ALU.mult, ALU.mult))
            E.op(lambda: ve.drain())
            if k >= 1:
                E.wait(("tr2", k - 1, 7))
            for tb in range(TB):
                f = (lambda tb=tb: ve.tensor_scalar(
                    sLNxA[:, tb, :], sx[:, k % 3, tb, :],
                    s_rb[:, tb:tb + 1], s_nmb[:, tb:tb + 1], ALU.mult, ALU.add))
                if tb < TB - 1:
                    E.op(f)
                else:
                    E.inc(s_v, f, ("c2", k))

        def v_split(E, ve, src, hi, lo, key):
            E.op(lambda: ve.tensor_copy(hi[:], src[:]))
            E.op(lambda: ve.drain())
            E.inc(s_sp, lambda: ve.tensor_tensor(lo[:], src[:], hi[:],
                                                 ALU.subtract), key)

        def v_scan(E, ve, k, i0, i1):
            if i0 == 0 and k % NBCH == 0:
                E.op(lambda: ve.memset(scarry[:], 0.0))
                E.op(lambda: ve.drain())
            for i in range(i0, i1):
                E.wait(("evU", k, i))
                E.inc(s_scan, lambda i=i: ve.tensor_tensor_scan(
                    sy_t[:, i, :], s_A[:, i:i + 1].broadcast_to([128, CH]),
                    su[:, i % 2, :], scarry[:, i:i + 1], ALU.mult, ALU.add),
                    ("scan", k, i))
            if i1 == KB:
                E.op(lambda: ve.drain())
                for i in range(KB):
                    E.op(lambda i=i: ve.tensor_copy(
                        scarry[:, i:i + 1], sy_t[:, i, CH - 1:CH]))
                E.op(lambda: ve.drain())

        def v_epiA(E, ve, k):
            if k == NBCH:
                E.wait(("gat",))
            for tb in range(TB):
                E.wait(("mm2", k, tb))
                for n in range(2):
                    E.inc(s_b47, lambda tb=tb, n=n: ve.tensor_tensor(
                        stmp[:],
                        pb[4 + 2 * tb + n][:],
                        s_GA[:, n * 512:(n + 1) * 512], ALU.mult),
                        ("epiAp", k, tb) if n == 1 else None)
                    E.op(lambda: ve.drain())
                    E.op(lambda tb=tb, n=n: ve.tensor_tensor(
                        sx[:, k % 3, tb, n * 512:(n + 1) * 512], stmp[:],
                        sx[:, k % 3, tb, n * 512:(n + 1) * 512], ALU.add))
                    E.op(lambda: ve.drain())
                f = (lambda tb=tb: ve.tensor_tensor(
                    sx[:, k % 3, tb, :], sx[:, k % 3, tb, :], s_GAB[:], ALU.add))
                if tb < TB - 1:
                    E.op(f)
                else:
                    E.inc(s_v, f, ("x2", k))
            E.op(lambda: ve.drain())

        def v_epiF(E, ve, k):
            E.op(lambda: ve.drain())
            for tb in range(TB):
                E.wait(("mm3", k, tb))
                for n in range(2):
                    E.inc(s_b47, lambda tb=tb, n=n: ve.tensor_tensor(
                        sy_flat[:, tb * D + n * 512:tb * D + (n + 1) * 512],
                        pb[4 + 2 * tb + n][:],
                        s_BIASC[:, n * 512:(n + 1) * 512], ALU.add),
                        ("epiFp", k, tb) if n == 1 else None)
            E.wait(("sq2", k))
            for tb in range(TB):
                E.wait(("tanh", k, tb))
                E.op(lambda tb=tb: ve.tensor_tensor(
                    sy_flat[:, tb * D:(tb + 1) * D],
                    sy_flat[:, tb * D:(tb + 1) * D], s_GB[:], ALU.mult))
                E.op(lambda: ve.drain())
                E.inc(s_out, lambda tb=tb: ve.tensor_tensor(
                    sx[:, k % 3, tb, :], sy_flat[:, tb * D:(tb + 1) * D],
                    sx[:, k % 3, tb, :], ALU.add),
                    ("out", k, tb))

        def t_TR(E, te, kind, k):
            E.wait(("c1" if kind == "tr1" else "c2", k))
            buf = sLNxB if kind == "tr1" else sLNxA
            for j in range(KB):
                E.wait(prev01[(kind, k, j)])
                for tb in range(TB):
                    f = (lambda j=j, tb=tb, buf=buf: te.transpose(
                        pb[j % 2][:, tb * 128:(tb + 1) * 128],
                        buf[:, tb, j * 128:(j + 1) * 128], s_id[:]))
                    if tb < TB - 1:
                        E.op(f)
                    else:
                        E.inc(s_tr, f, (kind, k, j))

        def t_MM1(E, te, k):
            E.wait(("spA", k))
            for i in range(KB):
                E.wait(prev23(k, i))
                p = pb[2 + (i % 2)]
                nmm = 3 * KB
                m = 0
                for wt, at in [(s_winT_hi, shi1), (s_winT_hi, slo1),
                               (s_winT_lo, shi1)]:
                    for kk in range(KB):
                        m += 1
                        f = (lambda i=i, kk=kk, wt=wt, at=at, p=p,
                             st=(m == 1), sp=(m == nmm): te.matmul(
                            p[:, 0:CH], wt[:, kk, i * 128:(i + 1) * 128],
                            at[:, kk, :], start=st, stop=sp))
                        if m < nmm:
                            E.op(f)
                        else:
                            E.inc(s_mm, f, ("mm1", k, i))

        def t_MM2(E, te, k):
            E.wait(("spY", k))
            for tb in range(TB):
                E.wait(("epiFp", k - 1, tb) if k >= 1 else None)
                m = 0
                nmm = 2 * 3 * KB
                for n in range(2):
                    p = pb[4 + 2 * tb + n]
                    first = True
                    for yt, wt in [(shiY, s_woutT_hi), (shiY, s_woutT_lo),
                                   (sloY, s_woutT_hi)]:
                        for kk in range(KB):
                            m += 1
                            f = (lambda tb=tb, n=n, kk=kk, yt=yt, wt=wt, p=p,
                                 st=first, sp=(yt is sloY and kk == KB - 1):
                                 te.matmul(p[:],
                                           yt[:, kk, tb * 128:(tb + 1) * 128],
                                           wt[:, kk, n * 512:(n + 1) * 512],
                                           start=st, stop=sp))
                            first = False
                            if m < nmm:
                                E.op(f)
                            else:
                                E.inc(s_mm, f, ("mm2", k, tb))

        def t_MM3(E, te, k):
            E.wait(("sp2", k))
            for tb in range(TB):
                E.wait(("epiAp", k, tb))
                m = 0
                nmm = 2 * 3 * KB
                for n in range(2):
                    p = pb[4 + 2 * tb + n]
                    first = True
                    for at, wt in [(shi2, s_Wc_hi), (shi2, s_Wc_lo),
                                   (slo2, s_Wc_hi)]:
                        for kk in range(KB):
                            m += 1
                            f = (lambda tb=tb, n=n, kk=kk, at=at, wt=wt, p=p,
                                 st=first, sp=(at is slo2 and kk == KB - 1):
                                 te.matmul(p[:],
                                           at[:, kk, tb * 128:(tb + 1) * 128],
                                           wt[:, kk, n * 512:(n + 1) * 512],
                                           start=st, stop=sp))
                            first = False
                            if m < nmm:
                                E.op(f)
                            else:
                                E.inc(s_mm, f, ("mm3", k, tb))

        def sc_SQ(E, sc, k, which):
            xap = lambda tb: sx[:, k % 3, tb, :]
            acc = s_s2a if which == 1 else s_s2b
            for tb in range(TB):
                if tb:
                    E.op(lambda: sc.drain())
                f = (lambda tb=tb: sc.activation(
                    sLNxm1[:, 0:4, :], xap(tb), AF.Square,
                    accum_out=acc[:, tb:tb + 1]))
                if tb < TB - 1:
                    E.op(f)
                else:
                    E.inc(s_sq, f, ("sq1" if which == 1 else "sq2", k))

        def sc_SQRT(E, sc, k, which):
            vin = s_vara if which == 1 else s_varb
            vout = s_sda if which == 1 else s_sdb
            E.wait(("vv1" if which == 1 else "vv2", k))
            E.inc(s_sq, lambda: sc.activation(vout[:], vin[:], AF.Sqrt),
                  ("sqrt1" if which == 1 else "sqrt2", k))

        def sc_evA(E, sc, k):
            bi = k // NBCH
            for j in range(KB):
                E.wait(("tr1", k, j))
                E.inc(s_b01, lambda j=j, bi=bi: sc.activation(
                    sLNxm1[:, j, :], pb[j % 2][:, 0:CH], AF.Identity,
                    scale=s_ca1[:, bi * 8 + j:bi * 8 + j + 1],
                    bias=s_sa[:, bi * 8 + j:bi * 8 + j + 1]),
                    ("evA", k, j))

        def sc_ev2(E, sc, k):
            bi = k // NBCH
            for j in range(KB):
                E.wait(("tr2", k, j))
                E.inc(s_b01, lambda j=j, bi=bi: sc.activation(
                    sLNxm2[:, j, :], pb[j % 2][:, 0:CH], AF.Identity,
                    scale=s_cb1[:, bi * 8 + j:bi * 8 + j + 1],
                    bias=s_sb[:, bi * 8 + j:bi * 8 + j + 1]),
                    ("ev2", k, j))

        def sc_evU(E, sc, k, i0=0, i1=KB):
            for i in range(i0, i1):
                E.wait(("mm1", k, i))
                if i >= 2:
                    E.wait(("scan", k, i - 2))
                elif k >= 1:
                    E.wait(("scan", k - 1, i + 6))
                E.inc(s_b23, lambda i=i: sc.activation(
                    su[:, i % 2, :], pb[2 + (i % 2)][:, 0:CH], AF.Identity,
                    scale=s_bv[:, i:i + 1], bias=s_bbi[:, i:i + 1]),
                    ("evU", k, i))

        def sc_TANH(E, sc, k):
            for tb in range(TB):
                E.wait(("epiFp", k, tb))
                if tb:
                    E.op(lambda: sc.drain())
                E.inc(s_th, lambda tb=tb: sc.activation(
                    sy_flat[:, tb * D:(tb + 1) * D],
                    sy_flat[:, tb * D:(tb + 1) * D], AF.Tanh),
                      ("tanh", k, tb))

        def stream_sync(E, sy):
            E.eng, E.engname = sy, "sync"
            for k in range(NCH):
                if k >= 3:
                    E.wait(("store", k - 3))
                if k >= 2:
                    E.wait(("xload", k - 2))
                E.inc([sem_x0, sem_x1][k % 2],
                      lambda k=k: sy.dma_start(out=sx[:, k % 3], in_=x_r[k]),
                      ("xload", k), by=16)

        def stream_gp(E, gp):
            E.eng, E.engname = gp, "gpsimd"
            for k in range(NCH):
                E.wait(("out", k, 1))
                if k >= 2:
                    E.wait(("store", k - 2))
                E.inc([st0, st1][k % 2],
                      lambda k=k: gp.dma_start(out=out_r[k], in_=sx[:, k % 3]),
                      ("store", k), by=16)
                if k == NBCH - 1:
                    for q, (slot, dst) in enumerate([(0, s_GA), (1, s_GAB),
                                                     (2, s_GB)]):
                        E.inc(sem_gat,
                              lambda slot=slot, dst=dst: gp.dma_start(
                                  out=dst[:], in_=gascr[slot]),
                              ("gat",) if q == 2 else None, by=16)
            E.wait(("store", NCH - 2))
            E.wait(("store", NCH - 1))

        def stream_T(E, te):
            E.eng, E.engname = te, "tensor"
            t_TR(E, te, "tr1", 0)
            for k in range(NPER):
                if k < NCH:
                    t_MM1(E, te, k)
                if 0 <= k - 1:
                    t_TR(E, te, "tr2", k - 1)
                if k + 1 < NCH:
                    t_TR(E, te, "tr1", k + 1)
                if 0 <= k - 1:
                    t_MM3(E, te, k - 1)
                if k < NCH:
                    t_MM2(E, te, k)

        def stream_V(E, ve):
            E.eng, E.engname = ve, "vector"
            v_LN1(E, ve, 0)
            E.wait(("evA", 0, 7))
            v_split(E, ve, sLNxm1, shi1, slo1, ("spA", 0))
            for k in range(NPER):
                if 0 <= k - 1:
                    v_epiA(E, ve, k - 1)
                if k < NCH:
                    v_scan(E, ve, k, 0, 2)
                if 0 <= k - 1:
                    v_LN2(E, ve, k - 1)
                if k < NCH:
                    v_scan(E, ve, k, 2, 4)
                if k + 1 < NCH:
                    v_LN1(E, ve, k + 1)
                if k < NCH:
                    v_scan(E, ve, k, 4, KB)
                if 0 <= k - 1:
                    E.wait(("ev2", k - 1, 7))
                    E.wait(("mm3", k - 2, 1) if k >= 2 else None)
                    v_split(E, ve, sLNxm2, shi2, slo2, ("sp2", k - 1))
                if k < NCH:
                    E.wait(("mm2", k - 1, 1) if k >= 1 else None)
                    v_split(E, ve, sy_t, shiY, sloY, ("spY", k))
                if k + 1 < NCH:
                    E.wait(("evA", k + 1, 7))
                    E.wait(("mm1", k, 7))
                    v_split(E, ve, sLNxm1, shi1, slo1, ("spA", k + 1))
                if 0 <= k - 1:
                    v_epiF(E, ve, k - 1)

        def stream_SC(E, sc):
            E.eng, E.engname = sc, "scalar"
            E.wait(("xload", 0))
            sc_SQ(E, sc, 0, 1)
            sc_SQRT(E, sc, 0, 1)
            sc_evA(E, sc, 0)
            for k in range(NPER):
                if k + 1 < NCH:
                    E.wait(("xload", k + 1))
                    E.wait(("vv1", k))
                    E.wait(("spA", k))
                    sc_SQ(E, sc, k + 1, 1)
                E.op(lambda: sc.drain())
                E.op(lambda: sc.activation(s_dum[:], s_A[:, 0:1], AF.Sqrt))
                if k < NCH:
                    sc_evU(E, sc, k, 0, 2)
                if 0 <= k - 1:
                    E.wait(("x2", k - 1))
                    E.wait(("vv2", k - 2) if k >= 2 else None)
                    sc_SQ(E, sc, k - 1, 2)
                if 0 <= k - 1:
                    sc_SQRT(E, sc, k - 1, 2)
                if k < NCH:
                    sc_evU(E, sc, k, 2, 4)
                if k + 1 < NCH:
                    sc_SQRT(E, sc, k + 1, 1)
                if k < NCH:
                    sc_evU(E, sc, k, 4, KB)
                if 0 <= k - 1:
                    if k < NCH:
                        E.wait(("spA", k))
                    sc_ev2(E, sc, k - 1)
                if k + 1 < NCH:
                    sc_evA(E, sc, k + 1)
                if 0 <= k - 1:
                    sc_TANH(E, sc, k - 1)

        E.plan = True
        stream_sync(E, None)
        stream_gp(E, None)
        stream_T(E, None)
        stream_V(E, None)
        stream_SC(E, None)

        E.plan = False
        EMIT_LOG.append(("__BLOCK__", "bar", "", 0))
        with nc.Block() as blk:
            @blk.sync
            def _(sy):
                stream_sync(E, sy)

            @blk.gpsimd
            def _(gp):
                stream_gp(E, gp)

            @blk.tensor
            def _(te):
                stream_T(E, te)

            @blk.vector
            def _(ve):
                stream_V(E, ve)

            @blk.scalar
            def _(sc):
                stream_SC(E, sc)

        if debug:
            dbg2 = [
                ("sx", sx[:, 0:2], F32), ("sLNx", sLNxA, F32), ("s_s1a", s_s1a, F32),
                ("s_s2a", s_s2a, F32), ("s_vara", s_vara, F32),
                ("s_sda", s_sda, F32), ("s_ra", s_ra, F32), ("s_nma", s_nma, F32),
                ("sLNxm", sLNxm1, F32), ("shiA", shi1, BF16), ("sloA", slo1, BF16),
                ("su", su, F32), ("sy_t", sy_t, F32), ("shiY", shiY, BF16),
                
                ("s_s1b", s_s1b, F32), ("s_varb", s_varb, F32),
                ("s_rb", s_rb, F32), ("scarry", scarry, F32),
            ]
            d2_outs = {}
            from concourse.bass import AP as _APc
            def _ap(t):
                return t if isinstance(t, _APc) else t[:]
            for nm, t, dt_ in dbg2:
                ap = _ap(t)
                d2_outs[nm] = nc.dram_tensor("dbg2_" + nm, list(ap.shape), dt_,
                                             kind="ExternalOutput").ap()
            EMIT_LOG.append(("__BLOCK__", "bar", "", 0))
            with nc.Block() as blk:
                @blk.gpsimd
                def _(gp):
                    n = 0
                    for nm, t, dt_ in dbg2:
                        I(gp.dma_start(out=d2_outs[nm], in_=_ap(t)), "gpsimd", u4, 16)
                        n += 1
                    W(gp, "gpsimd", u4, G["u4"] + 16 * n)
            G["u4"] += 16 * len(dbg2)

    return nc


def kernel(**inputs):
    x = np.ascontiguousarray(np.asarray(inputs["L0_attn_x"], dtype=np.float32))
    w_in = np.asarray(inputs["L0_attn_w_in"], dtype=np.float32)
    w_out = np.asarray(inputs["L0_attn_w_out"], dtype=np.float32)
    wff = np.asarray(inputs["L0_ffn_weight"], dtype=np.float32)
    a1 = np.ascontiguousarray(np.asarray(inputs["L0_ffn_aux_weight1"], dtype=np.float32))
    a2 = np.ascontiguousarray(np.asarray(inputs["L0_ffn_aux_weight2"], dtype=np.float32))
    wmod = np.ascontiguousarray(np.asarray(inputs["L0_wmod"], dtype=np.float32))
    cmat = np.asarray(inputs["L0_c"], dtype=np.float32)

    winT = np.ascontiguousarray(w_in.T)
    woutT = np.ascontiguousarray(w_out.T)
    wT = np.ascontiguousarray(wff.T)
    ident = np.eye(128, dtype=np.float32)
    ones1 = np.ones((1, 128), dtype=np.float32)
    avec = np.ascontiguousarray(np.asarray(inputs["L0_attn_a"], dtype=np.float32))
    bvec = np.ascontiguousarray(np.asarray(inputs["L0_attn_b"], dtype=np.float32))
    binv = np.ascontiguousarray(np.asarray(inputs["L0_attn_b_in"], dtype=np.float32))
    boutv = np.ascontiguousarray(np.asarray(inputs["L0_attn_b_out"], dtype=np.float32))
    fbias = np.ascontiguousarray(np.asarray(inputs["L0_ffn_bias"], dtype=np.float32))

    nc = build_program()
    in_maps = []
    for core in range(8):
        xc = np.ascontiguousarray(x[2 * core:2 * core + 2].reshape(TC, D))
        cT = np.ascontiguousarray(cmat[2 * core:2 * core + 2].T)
        in_maps.append(dict(
            x=xc, winT=winT, woutT=woutT, wT=wT, a1=a1, a2=a2, wmod=wmod,
            cT=cT, avec=avec, bvec=bvec, binv=binv, boutv=boutv, fbias=fbias,
            ident=ident, ones1=ones1,
        ))
    res = run_bass_kernel_spmd(nc, in_maps, core_ids=list(range(8)))
    out = np.concatenate([r["out"] for r in res.results], axis=0)
    return out.reshape(16, 4096, D).astype(np.float32)
